# version 04
# baseline (speedup 1.0000x reference)
import contextlib
import numpy as np

import concourse.bass as bass
import concourse.mybir as mybir
from concourse.bass_utils import run_bass_kernel_spmd

F32 = mybir.dt.float32
BF16 = mybir.dt.bfloat16
AF = mybir.ActivationFunctionType
ALU = mybir.AluOpType

D, E, HID, NST = 1024, 2048, 4096, 16
DBG_TAPS = False
DBG2 = False
TAPC = 1
FC, ECN, HCN = 8, 16, 32
TC = 512
P = 128
NCH_FULL = 16
EPS = 1e-5
INV_D = 1.0 / D

GROUP_K = {
    "w_in": 8, "w_b": 16, "w_c": 16, "gate": 8, "w_delta": 16,
    "w1e0": 8, "w1e1": 8, "sw1": 8,
    "w2e0h0": 16, "w2e0h1": 16, "w2e1h0": 16, "w2e1h1": 16,
    "sw2h0": 16, "sw2h1": 16, "w_out": 16,
}
MAIN_STAGES = {"w_in", "w_delta", "w1e0", "w1e1", "sw1", "w2e0h0", "w2e0h1",
               "w2e1h0", "w2e1h1", "sw2h0", "sw2h1", "w_out"}
W1_OF = {"e0": "w1e0", "e1": "w1e1", "sh": "sw1"}
W2_OF = {"e0": "w2e0", "e1": "w2e1", "sh": "sw2"}
H1_PREV = {("w1e0", 1): "w2e0h0", ("w1e1", 0): "w2e0h1", ("w1e1", 1): "w2e1h0",
           ("sw1", 0): "w2e1h1", ("sw1", 1): "sw2h0"}


def pe_group_sequence():
    seq = [("w_in", oc) for oc in range(32)]
    seq += [("w_b", 0), ("w_c", 0), ("gate", 0), ("gate", 1)]
    ffn = []
    for path in ("e0", "e1", "sh"):
        for half in range(2):
            ffn += [(W1_OF[path], oc) for oc in range(16 * half, 16 * half + 16)]
            ffn += [(W2_OF[path] + f"h{half}", oc) for oc in range(8)]
    inter = [("w_delta", 0), ("w_delta", 1), ("w_delta", 2)]
    nxt = 3
    total_t = sum(GROUP_K[g[0]] for g in ffn)
    cum = 0
    for i, g in enumerate(ffn):
        inter.append(g)
        cum += GROUP_K[g[0]]
        if nxt < ECN and cum % 16 == 0 and cum >= (nxt - 2) * total_t / 13.0:
            inter.append(("w_delta", nxt))
            nxt += 1
    assert nxt == ECN, nxt
    seq += inter
    seq += [("w_out", oc) for oc in range(8)]
    return seq


PE_GROUPS = pe_group_sequence()
MAIN_GROUPS = [g for g in PE_GROUPS if g[0] in MAIN_STAGES]
NMAIN = len(MAIN_GROUPS)
G_ROT = {g: i for i, g in enumerate(MAIN_GROUPS)}
NONWD = [g for g in MAIN_GROUPS if g[0] != "w_delta"]
R2 = {g: i for i, g in enumerate(NONWD)}
NR2 = len(NONWD)
GPOS = {g: i for i, g in enumerate(PE_GROUPS)}
WD_POS = {oc: GPOS[("w_delta", oc)] for oc in range(ECN)}
FFN_STAGES = {"w1e0", "w1e1", "sw1", "w2e0h0", "w2e0h1", "w2e1h0", "w2e1h1",
              "sw2h0", "sw2h1"}


def wtile_order():
    L = []
    for stage, oc in PE_GROUPS:
        for kc in range(GROUP_K[stage]):
            L.append((stage, oc, kc))
    assert len(L) % 16 == 0
    return L


WTILES = wtile_order()
NT = len(WTILES)
NSLOT = NT // 16
RING = 2
TEN_Y0 = 38
SL = 2048
NPREP = NT * P // SL
assert NPREP * SL == NT * P


def host_pack_wtiles(inp):
    ws = {
        "w_in": inp["L0_attn_w_in"], "w_b": inp["L0_attn_w_b"],
        "w_c": inp["L0_attn_w_c"], "w_delta": inp["L0_attn_w_delta"],
        "w_out": inp["L0_attn_w_out"], "gate_w": inp["L0_ffn_gate_w"],
        "w1": inp["L0_ffn_w1"], "w2": inp["L0_ffn_w2"],
        "sw1": inp["L0_ffn_sw1"], "sw2": inp["L0_ffn_sw2"],
    }
    ws = {k: np.asarray(v) for k, v in ws.items()}
    arr = np.zeros((NT, P, P), dtype=np.float32)
    for i, (stage, oc, kc) in enumerate(WTILES):
        if stage == "w_in":
            arr[i] = ws["w_in"][kc * P:(kc + 1) * P, oc * P:(oc + 1) * P]
        elif stage == "w_b":
            arr[i, :, 0:NST] = ws["w_b"][kc * P:(kc + 1) * P, :]
        elif stage == "w_c":
            arr[i, :, 0:NST] = ws["w_c"][kc * P:(kc + 1) * P, :]
        elif stage == "gate":
            arr[i, :, 0] = ws["gate_w"][oc, kc * P:(kc + 1) * P]
        elif stage == "w_delta":
            arr[i] = ws["w_delta"][kc * P:(kc + 1) * P, oc * P:(oc + 1) * P]
        elif stage in ("w1e0", "w1e1", "sw1"):
            w = {"w1e0": ws["w1"][0], "w1e1": ws["w1"][1], "sw1": ws["sw1"]}[stage]
            arr[i] = w[kc * P:(kc + 1) * P, oc * P:(oc + 1) * P]
        elif stage[:-2] in ("w2e0", "w2e1", "sw2"):
            w = {"w2e0": ws["w2"][0], "w2e1": ws["w2"][1], "sw2": ws["sw2"]}[stage[:-2]]
            kcg = int(stage[-1]) * 16 + kc
            arr[i] = w[kcg * P:(kcg + 1) * P, oc * P:(oc + 1) * P]
        elif stage == "w_out":
            arr[i] = ws["w_out"][kc * P:(kc + 1) * P, oc * P:(oc + 1) * P]
    return np.ascontiguousarray(arr.transpose(1, 0, 2).reshape(P, NT * P))


SM = {}
_off = 0
for _n, _w in [("wnorm", FC), ("bnorm", FC), ("w4", ECN * 4), ("bconv", ECN),
               ("bdelta", ECN), ("Dcol", ECN), ("Alog", ECN * NST),
               ("b1e0", HCN), ("b1e1", HCN), ("sb1", HCN),
               ("b2e0", FC), ("b2e1", FC), ("sb2", FC), ("bsum", FC)]:
    SM[_n] = (_off, _off + _w)
    _off += _w
SMW = _off


def host_pack_smalls(inp):
    sm = np.zeros((P, SMW), dtype=np.float32)

    def put(name, vec, w):
        a, b = SM[name]
        sm[:, a:b] = np.asarray(vec).reshape(w, P).T

    put("wnorm", inp["L0_attn_w_norm"], FC)
    put("bnorm", inp["L0_attn_b_norm"], FC)
    a, b = SM["w4"]
    sm[:, a:b] = np.asarray(inp["L0_attn_w_conv"]).reshape(ECN, P, 4).transpose(1, 0, 2).reshape(P, ECN * 4)
    put("bconv", inp["L0_attn_b_conv"], ECN)
    put("bdelta", inp["L0_attn_b_delta"], ECN)
    put("Dcol", inp["L0_attn_D_param"], ECN)
    a, b = SM["Alog"]
    sm[:, a:b] = np.asarray(inp["L0_attn_A_log"]).reshape(ECN, P, NST).transpose(1, 0, 2).reshape(P, ECN * NST)
    put("b1e0", inp["L0_ffn_b1"][0], HCN)
    put("b1e1", inp["L0_ffn_b1"][1], HCN)
    put("sb1", inp["L0_ffn_sb1"], HCN)
    put("b2e0", inp["L0_ffn_b2"][0], FC)
    put("b2e1", inp["L0_ffn_b2"][1], FC)
    put("sb2", inp["L0_ffn_sb2"], FC)
    return sm


class Prog:
    def __init__(self, dry, events=None, sems=None):
        self.dry = dry
        self.cnt = {k: 0 for k in ("sy", "pe", "ve", "se", "po", "pd")}
        self.events = events if events is not None else {}
        self.sems = sems or {}
        self.waited = {}

    def inc(self, instr, sem, key, n=1):
        self.cnt[sem] += n
        ev = (sem, self.cnt[sem])
        if self.dry:
            assert key not in self.events, f"dup event {key}"
            self.events[key] = ev
        else:
            assert self.events.get(key) == ev, (key, self.events.get(key), ev)
            if instr is not None:
                instr.then_inc(self.sems[sem], n)

    def alias(self, newkey, oldkey):
        if self.dry:
            self.events[newkey] = self.events[oldkey]

    def wait(self, engname, eng, key):
        if self.dry:
            return
        sem, val = self.events[key]
        w = self.waited.setdefault(engname, {})
        if w.get(sem, -1) >= val:
            return
        w[sem] = val
        eng.wait_ge(self.sems[sem], val)


def build_nc(nch=NCH_FULL):
    T = nch * TC
    nc = bass.Bass("TRN2", target_bir_lowering=False)

    xT = nc.dram_tensor("xT", [D, T], F32, kind="ExternalInput").ap()
    wt = nc.dram_tensor("wt", [P, NT * P], F32, kind="ExternalInput").ap()
    smI = nc.dram_tensor("smI", [P, SMW], F32, kind="ExternalInput").ap()
    outT = nc.dram_tensor("outT", [D, T], F32, kind="ExternalOutput").ap()
    dbg = nc.dram_tensor("dbg", [P, 8 * 8192], BF16, kind="ExternalOutput").ap() if DBG_TAPS else None
    dbg2 = nc.dram_tensor("dbg2", [P, 4096], F32, kind="ExternalOutput").ap() if DBG2 else None
    wsb = nc.dram_tensor("wsb", [P, NT * P], BF16, kind="Internal").ap()

    ctx = contextlib.ExitStack()
    with ctx:
        sb = lambda name, shape, dt: ctx.enter_context(nc.sbuf_tensor(name, shape, dt))
        wring = sb("wring", [P, RING * 16 * P], BF16)
        xbuf = sb("xbuf", [P, FC * TC], F32)
        xnb = sb("xnb", [P, FC * TC], BF16)
        xlnmx = sb("xlnmx", [P, FC * TC], BF16)
        pool = sb("pool", [P, ECN * 516], BF16)
        szb = sb("szb", [P, ECN * TC], BF16)
        xcyg = sb("xcyg", [P, ECN * TC], BF16)
        deltar = sb("deltar", [P, 3 * TC], BF16)
        brep = sb("brep", [P, NST * TC], BF16)
        crep = sb("crep", [P, NST * TC], BF16)
        bpad = sb("bpad", [P, TC], BF16)
        cpad = sb("cpad", [P, TC], BF16)
        aring = sb("aring", [P, 8 * TC], BF16)
        b4r = sb("b4r", [P, 2 * 4 * TC], BF16)
        q4r = sb("q4r", [P, 2 * 4 * TC], BF16)
        h4r = sb("h4r", [P, 2 * 4 * TC], BF16)
        ur = sb("ur", [P, 2 * TC], BF16)
        gacc = sb("gacc", [P, FC * TC], BF16)
        sres = sb("sres", [P, FC * TC], BF16)
        carry = sb("carry", [P, ECN * NST], F32)
        smalls = sb("smalls", [P, SMW], F32)
        acols = sb("acols", [P, ECN * NST], F32)
        selt = sb("selt", [P, NST * P], BF16)
        ident = sb("ident", [P, P], BF16)
        onescf = sb("onescf", [P, P], F32)
        onescb = sb("onescb", [P, P], BF16)
        onesrf = sb("onesrf", [1, P], F32)
        maskr = sb("maskr", [P, 2 * TC], BF16)
        repf = sb("repf", [P, 2 * TC], F32)
        scrA = sb("scrA", [P, 2 * TC], F32)
        scrSP = sb("scrSP", [P, 3 * TC], BF16)
        scrY = sb("scrY", [P, TC], F32)
        scrQ = sb("scrQ", [P, 2 * TC], BF16)
        rows = sb("rows", [1, 3 * TC], F32)
        iop = sb("iop", [P, 1], F32)
        consts = sb("consts", [P, 4], F32)

        psM = [ctx.enter_context(nc.psum_tensor(f"psM{i}", [P, TC], F32)) for i in range(4)]
        psY = [ctx.enter_context(nc.psum_tensor(f"psY{i}", [P, TC], F32)) for i in range(2)]
        psA = ctx.enter_context(nc.psum_tensor("psA", [P, TC], F32))
        psB = ctx.enter_context(nc.psum_tensor("psB", [P, TC], F32))

        sems = {k: ctx.enter_context(nc.semaphore(k))
                for k in ("sy", "pe", "ve", "se", "po", "pd")}
        for _i, _v in enumerate((EPS, 1.0, -0.5, 0.0)):
            nc.const_aps.register(F32, _v, consts[:, _i:_i + 1])
        block = ctx.enter_context(nc.Block())

        xv = lambda fc: xbuf[:, fc * TC:(fc + 1) * TC]
        xnv = lambda fc: xnb[:, fc * TC:(fc + 1) * TC]
        xlnv = lambda fc: xlnmx[:, fc * TC:(fc + 1) * TC]
        szv = lambda ec: szb[:, ec * TC:(ec + 1) * TC]
        xcv = lambda ec: xcyg[:, ec * TC:(ec + 1) * TC]
        gv = lambda fc: gacc[:, fc * TC:(fc + 1) * TC]
        sv = lambda fc: sres[:, fc * TC:(fc + 1) * TC]
        poolv = lambda i, c0, c1: pool[:, i * 516 + c0: i * 516 + c1]
        brv = lambda n: brep[:, n * TC:(n + 1) * TC]
        crv = lambda n: crep[:, n * TC:(n + 1) * TC]
        arv = lambda s: aring[:, s * TC:(s + 1) * TC]
        b4v = lambda s: b4r[:, s * 4 * TC:(s + 1) * 4 * TC]
        h4v = lambda s: h4r[:, s * 4 * TC:(s + 1) * 4 * TC]
        q4v = lambda s: q4r[:, s * 4 * TC:(s + 1) * 4 * TC]
        uv = lambda s: ur[:, s * TC:(s + 1) * TC]
        scrSPb = scrA[:].bitcast(BF16)
        spv = lambda j: scrSPb[:, j * TC:(j + 1) * TC]
        dlv = lambda ec: deltar[:, (ec % 3) * TC:(ec % 3 + 1) * TC]
        scrAv = lambda s: scrA[:, s * TC:(s + 1) * TC]
        scrYv = lambda s: scrY[:, 0:TC]
        xsqv = lambda s: xnb[:].bitcast(F32)[:, (s % 4) * TC:(s % 4 + 1) * TC]
        scrQv = lambda s: xlnmx[:, (s % 4) * TC:(s % 4 + 1) * TC]
        rowv = lambda i: rows[0:1, i * TC:(i + 1) * TC]
        maskv = lambda i: maskr[:, i * TC:(i + 1) * TC]
        repv = lambda i: repf[:, i * TC:(i + 1) * TC]
        selv = lambda n: selt[:, n * P:(n + 1) * P]

        def smc(name, j=0, w=1):
            a, b = SM[name]
            assert a + j + w <= b
            return smalls[:, a + j:a + j + w]

        def ring_tile(t):
            tt = t % NT
            s = (t // 16) % RING
            return wring[:, (s * 16 + tt % 16) * P:(s * 16 + tt % 16 + 1) * P]

        TAP_NXT = {("w1e0", 0): ("w1e0", 1), ("w1e0", 1): ("w1e1", 0),
                   ("w1e1", 0): ("w1e1", 1), ("w1e1", 1): ("sw1", 0),
                   ("sw1", 0): ("sw1", 1), ("sw1", 1): None}
        TAP_SLOT = {}
        if DBG_TAPS:
            last_tile = {}
            for i, (st_, oc_, kc_) in enumerate(WTILES):
                last_tile[(st_, oc_)] = i
            for pi, (st_, half) in enumerate([("w1e0", 0), ("w1e0", 1), ("w1e1", 0),
                                              ("w1e1", 1), ("sw1", 0), ("sw1", 1)]):
                TAP_SLOT[last_tile[(st_, half * 16 + 15)] // 16] = (pi, st_, half)
        xTr = xT.rearrange("(f p) t -> p f t", p=P)
        outTr = outT.rearrange("(f p) t -> p f t", p=P)

        def sgkey(gg):
            c2, r = divmod(gg, 4 * ECN)
            return ("h4", c2, r // 4, r % 4)

        def q4key(gg):
            c2, r = divmod(gg, 4 * ECN)
            return ("q4", c2, r // 4, r % 4)

        def peygkey(gg):
            c2, r = divmod(gg, 4 * ECN)
            return ("peyg", c2, r // 4, r % 4)

        def akey(gan):
            c2, r = divmod(gan, NST * ECN)
            return ("a", c2, r // NST, r % NST)

        def emit_sync(Pr, eng):
            wait = lambda k: Pr.wait("sync", eng, k)

            def dma(dst, src, key):
                instr = None if Pr.dry else eng.dma_start(out=dst, in_=src)
                Pr.inc(instr, "sy", key, 16)

            dma(smalls[:], smI, ("sm",))
            for i in range(NPREP):
                if i > 0:
                    wait(("prepcast", i - 1))
                dma(xbuf[:, 0:SL], wt[:, i * SL:(i + 1) * SL], ("prepin", i))
            for c in range(nch):
                if c == 0:
                    wait(("prepout", NPREP - 1))
                dma(xbuf[:], xTr[:, :, c * TC:(c + 1) * TC], ("xin", c))
                for s in range(NSLOT):
                    g = c * NSLOT + s
                    if g >= RING:
                        wait(("wfree_g", g - RING))
                    dma(wring[:, (g % RING) * 16 * P:(g % RING + 1) * 16 * P],
                        wsb[:, s * 16 * P:(s + 1) * 16 * P], ("wslot", c, s))
                    if s == 12 and c >= 1:
                        wait(("xn", c))
                        wait(("peSx", c))
                        wait(("xsq", c, FC - 1))
                        dma(xbuf[:], xTr[:, :, (c - 1) * TC:c * TC], ("xre", c - 1))
                        wait(("combine", c - 1))
                        dma(outTr[:, :, (c - 1) * TC:c * TC], xbuf[:], ("outdma", c - 1))
                    if DBG_TAPS and c == TAPC and s in TAP_SLOT:
                        pi, st_, half = TAP_SLOT[s]
                        if (st_, half) == ("mx",):
                            pass
                        for kc in range(16):
                            r = G_ROT[(st_, half * 16 + kc)]
                            wait(("bankfree", TAPC, r))
                            nxt = TAP_NXT[(st_, half)]
                            key = ("tapdma", nxt, kc) if nxt else ("tapend", kc)
                            dma(dbg[:, (pi * 16 + kc) * TC:(pi * 16 + kc + 1) * TC],
                                poolv(kc, 0, TC), key)
                        if pi == 0:
                            wait(("mx0", TAPC))
                            dma(dbg[:, 96 * TC:104 * TC], xlnmx[:], ("tapmx0",))
                        if pi == 2:
                            wait(("mx1", TAPC))
                            dma(dbg[:, 104 * TC:112 * TC], xlnmx[:], ("tapmx1",))

            wait(("xn", nch - 1))
            wait(("peSx", nch - 1))
            wait(("xsq", nch - 1, FC - 1))
            dma(xbuf[:], xTr[:, :, (nch - 1) * TC:nch * TC], ("xre", nch - 1))
            wait(("combine", nch - 1))
            dma(outTr[:, :, (nch - 1) * TC:nch * TC], xbuf[:], ("outdma", nch - 1))
            if not Pr.dry:
                eng.wait_ge(Pr.sems["sy"], Pr.cnt["sy"])

        def emit_gpsimd(Pr, eng):
            wait = lambda k: Pr.wait("gpsimd", eng, k)
            dry = Pr.dry
            if not dry:
                eng.iota(iop[:], [[0, 1]], channel_multiplier=1,
                         allow_small_or_imprecise_dtypes=True)
                i2 = eng.iota(scrA[:, 0:P], [[1, P]], channel_multiplier=0,
                              allow_small_or_imprecise_dtypes=True)
            else:
                i2 = None
            Pr.inc(i2, "po", ("iotas",), 1)
            for i in range(NPREP):
                wait(("prepcast", i))
                instr = None if dry else eng.dma_start(
                    out=wsb[:, i * SL:(i + 1) * SL], in_=xnb[:, 0:SL])
                Pr.inc(instr, "pd", ("prepout", i), 16)
            def tap2(i, w, src_ap, key):
                if not DBG2:
                    return
                wait(w)
                instr = None if dry else eng.dma_start(
                    out=dbg2[:, i * 512:(i + 1) * 512], in_=src_ap)
                Pr.inc(instr, "pd", ("t2", key), 16)

            for c in range(nch):
                if DBG2 and c == 1:
                    tap2(0, ("tail", 1, 0), pool[:, 0:512], "tail")
                    if DBG2:
                        wait(("h4", 0, ECN - 1, 3))
                        instr = None if dry else eng.dma_start(
                            out=dbg2[:, 4 * 512:4 * 512 + 256], in_=carry[:, 0:256])
                        Pr.inc(instr, "pd", ("t2", "carry"), 16)
                    tap2(1, ("bankfree", 1, G_ROT[("w_in", 0)]), pool[:, 0:512], "xpev")
                    tap2(2, ("conv", 1, 0), scrA[:, 0:512], "cv")
                    tap2(3, ("xc", 1, 0), xcyg[:, 0:512], "xc")
                    tap2(4, ("h4", 0, ECN - 1, 3), carry[:, 0:256], "carry") if False else None
                    tap2(5, ("delta", 1, 0), deltar[:, 0:512], "delta")
                    tap2(6, ("a", 1, 0, 3), aring[:, 0:512], "a0")
                def po_b4(ec, ng):
                    g = ec * 4 + ng
                    gg = c * 4 * ECN + g
                    slot = g % 2
                    if gg >= 2:
                        wait(sgkey(gg - 2))
                    for n in range(ng * 4, ng * 4 + 4):
                        wait(("brevac", c, n))
                    u_b = uv(ec % 2).unsqueeze(1).broadcast_to([P, 4, TC])
                    instr = None if dry else eng.tensor_mul(
                        b4v(slot).rearrange("p (g t) -> p g t", g=4), u_b,
                        brep[:, ng * 4 * TC:(ng + 1) * 4 * TC].rearrange(
                            "p (g t) -> p g t", g=4))
                    Pr.inc(instr, "po", ("b4", c, ec, ng), 1)

                def po_q4(ec, ng):
                    g = ec * 4 + ng
                    gg = c * 4 * ECN + g
                    slot = g % 2
                    wait(("h4", c, ec, ng))
                    for n in range(ng * 4, ng * 4 + 4):
                        wait(("crevac", c, n))
                    if gg >= 2:
                        wait(peygkey(gg - 2))
                    instr = None if dry else eng.tensor_mul(
                        q4v(slot).rearrange("p (g t) -> p g t", g=4),
                        h4v(slot).rearrange("p (g t) -> p g t", g=4),
                        crep[:, ng * 4 * TC:(ng + 1) * 4 * TC].rearrange(
                            "p (g t) -> p g t", g=4))
                    Pr.inc(instr, "po", ("q4", c, ec, ng), 1)

                for ec in range(ECN):
                    wait(("u", c, ec))
                    po_b4(ec, 0)
                    po_b4(ec, 1)
                    po_q4(ec, 0)
                    po_b4(ec, 2)
                    po_q4(ec, 1)
                    po_b4(ec, 3)
                    po_q4(ec, 2)
                    po_q4(ec, 3)

        def emit_tensor(Pr, eng):
            wait = lambda k: Pr.wait("tensor", eng, k)
            dry = Pr.dry

            def mm(ps, lhsT, rhs, start, stop, inckey=None):
                instr = None if dry else eng.matmul(ps, lhsT, rhs, start=start, stop=stop)
                if inckey is not None:
                    Pr.inc(instr, "pe", inckey, 1)
                return instr

            tcur = [0]

            def take_tile(c):
                t = tcur[0]
                tcur[0] += 1
                if t % 16 == 0:
                    wait(("wslot", c, (t % NT) // 16))
                return ring_tile(t)

            wait(("ones",))
            wait(("selinit",))
            for c in range(nch):
                assert tcur[0] == c * NT
                wait(("xin", c))
                if c > 0:
                    wait(("bankfree", c - 1, G_ROT[("w_delta", ECN - 2)]))
                    wait(("bankfree", c - 1, G_ROT[("w_delta", ECN - 1)]))
                for fc in range(FC):
                    mm(psA[:], onescf[:], xv(fc), fc == 0, fc == FC - 1,
                       ("peSx", c) if fc == FC - 1 else None)
                for fc in range(FC):
                    wait(("xsq", c, fc))
                    mm(psB[:], onescf[:], xsqv(fc), fc == 0, fc == FC - 1,
                       ("peSx2f", c, fc))
                Pr.alias(("peSx2", c), ("peSx2f", c, FC - 1))
                wait(("rstd1", c))
                if c > 0:
                    wait(("psyfree", c - 1, TEN_Y0 + ECN - 2))
                mm(psY[0][:], onesrf[:], rowv(1), True, True, ("pebc", c, 0))
                wait(("mr1", c))
                if c > 0:
                    wait(("psyfree", c - 1, TEN_Y0 + ECN - 1))
                mm(psY[1][:], onesrf[:], rowv(0), True, True, ("pebc", c, 1))
                wait(("var1", c))
                for fc in range(FC):
                    wait(("xnfc", c, fc))
                    mm(psA[:], onescb[:], xnv(fc), fc == 0, fc == FC - 1,
                       ("peSxn", c) if fc == FC - 1 else None)
                for fc in range(FC):
                    wait(("xnsq", c, fc))
                    mm(psB[:], onescb[:], scrQv(fc), fc == 0, fc == FC - 1,
                       ("peSxn2f", c, fc))
                Pr.alias(("peSxn2", c), ("peSxn2f", c, FC - 1))
                wait(("rstd2", c))
                wait(("psyfree", c, 0))
                mm(psY[0][:], onesrf[:], rowv(1), True, True, ("pebc", c, 2))
                wait(("mr2", c))
                wait(("psyfree", c, 1))
                mm(psY[1][:], onesrf[:], rowv(0), True, True, ("pebc", c, 3))

                yg_next = [0]

                def emit_y_ng():
                    g = yg_next[0]
                    if g >= 4 * ECN:
                        return False
                    ec, ng = g // 4, g % 4
                    ten = TEN_Y0 + ec
                    bank = ec % 2
                    if ng == 0:
                        wait(("psyfree", c, ten - 2))
                    wait(("q4", c, ec, ng))
                    slot = g % 2
                    for j in range(4):
                        n = ng * 4 + j
                        key = None
                        if n == NST - 1:
                            key = ("pey", c, ec)
                        elif j == 3:
                            key = ("peyg", c, ec, ng)
                        mm(psY[bank][:], ident[:], q4v(slot)[:, j * TC:(j + 1) * TC],
                           n == 0, n == NST - 1, key)
                    if ng == 3:
                        Pr.alias(("peyg", c, ec, 3), ("pey", c, ec))
                    yg_next[0] += 1
                    return True

                for stage, oc in PE_GROUPS:
                    K = GROUP_K[stage]
                    gkey = ("peg", c, stage, oc)
                    ps = None
                    if stage == "w_delta":
                        r = G_ROT[(stage, oc)]
                        if oc >= 2:
                            wait(("bankfree", c, G_ROT[("w_delta", oc - 2)]))
                        elif oc == 0:
                            wait(("gatefree0", c))
                        else:
                            wait(("gatefree1", c))
                        ps = psA if oc % 2 == 0 else psB
                    elif stage in MAIN_STAGES:
                        r = G_ROT[(stage, oc)]
                        r2 = R2[(stage, oc)]
                        if r2 >= 4:
                            wait(("bankfree", c, G_ROT[NONWD[r2 - 4]]))
                        elif c > 0:
                            wait(("bankfree", c - 1, G_ROT[NONWD[NR2 - 4 + r2]]))
                        ps = psM[r2 % 4]
                    if stage == "w_in":
                        wait(("xln", c))
                        rhs = [xlnv(kc) for kc in range(FC)]
                    elif stage == "w_b":
                        for kc in range(ECN):
                            wait(("xc", c, kc))
                        wait(("var2", c))
                        rhs = [xcv(kc) for kc in range(ECN)]
                        ps = psA
                    elif stage == "w_c":
                        wait(("var2", c))
                        rhs = [xcv(kc) for kc in range(ECN)]
                        ps = psB
                    elif stage == "gate":
                        rhs = [xnv(kc) for kc in range(FC)]
                        ps = psA if oc == 0 else psB
                        wait(("bpadev", c) if oc == 0 else ("cpadev", c))
                    elif stage == "w_delta":
                        rhs = [xcv(kc) for kc in range(ECN)]
                    elif stage == "w1e0":
                        wait(("mx0", c))
                        rhs = [xlnv(kc) for kc in range(FC)]
                    elif stage == "w1e1":
                        wait(("mx1", c))
                        rhs = [xlnv(kc) for kc in range(FC)]
                    elif stage == "sw1":
                        rhs = [xnv(kc) for kc in range(FC)]
                    elif stage in ("w2e0h0", "w2e0h1", "w2e1h0", "w2e1h1",
                                   "sw2h0", "sw2h1"):
                        rhs = []
                        for kc in range(16):
                            wait(("h1", c, stage, kc))
                            rhs.append(poolv(kc, 0, TC))
                    elif stage == "w_out":
                        while yg_next[0] < 4 * ECN:
                            emit_y_ng()
                        for ec in range(ECN):
                            wait(("yg", c, ec))
                        rhs = [szv(kc) for kc in range(ECN)]
                    for kc in range(K):
                        lt = take_tile(c)
                        mm(ps[:], lt, rhs[kc], kc == 0, kc == K - 1,
                           gkey if kc == K - 1 else None)
                    if tcur[0] % 16 == 0:
                        Pr.alias(("wfree_g", (tcur[0] - 1) // 16), gkey)

                    if stage == "w_c":
                        wait(("bpadev", c))
                        for n in range(NST):
                            if n >= 2:
                                wait(("psyfree", c, 4 + n - 2))
                            else:
                                wait(("psyfree", c, 2 + n))
                            mm(psY[n % 2][:], selv(n), bpad[:], True, True,
                               ("pebr", c, n))
                        wait(("cpadev", c))
                        for n in range(NST):
                            wait(("psyfree", c, 20 + n - 2))
                            mm(psY[n % 2][:], selv(n), cpad[:], True, True,
                               ("pecr", c, n))
                    if stage == "gate" and oc == 1:
                        wait(("mask0", c))
                        wait(("psyfree", c, 34))
                        mm(psY[0][:], onesrf[:], rowv(2), True, True, ("pebc", c, 4))
                        wait(("mask1", c))
                        wait(("psyfree", c, 35))
                        mm(psY[1][:], onesrf[:], rowv(0), True, True, ("pebc", c, 5))
                    if stage == "w_delta" and oc >= 1:
                        target = 4 * (oc - 1)
                        while yg_next[0] < min(target, 4 * ECN):
                            if not emit_y_ng():
                                break
                while yg_next[0] < 4 * ECN:
                    emit_y_ng()

        def emit_yevac(Pr, eng, op, wait, c, ec):
            wait(("pey", c, ec))
            wait(("bankfree", c, G_ROT[("w_in", ECN + ec)]))
            bank = ec % 2
            op("stt", scrYv(ec % 2), xcv(ec), smc("Dcol", ec), psY[bank][:],
               op0=ALU.mult, op1=ALU.add)
            op("stt", szv(ec), scrYv(ec % 2), 1.0, szv(ec), op0=ALU.mult,
               op1=ALU.mult, inckey=("yg", c, ec))
            Pr.alias(("psyfree", c, TEN_Y0 + ec), ("yg", c, ec))

        def emit_vector(Pr, eng):
            wait = lambda k: Pr.wait("vector", eng, k)
            dry = Pr.dry

            def op(kind, *a, inckey=None, **kw):
                if dry:
                    instr = None
                else:
                    f = {"stt": eng.scalar_tensor_tensor, "ts": eng.tensor_scalar,
                         "cp": eng.tensor_copy, "scan": eng.tensor_tensor_scan,
                         "ms": eng.memset}[kind]
                    instr = f(*a, **kw)
                if inckey is not None:
                    Pr.inc(instr, "ve", inckey, 1)
                return instr

            wait(("iotas",))
            wait(("sm",))
            for n in range(NST):
                op("ts", selv(n), iop[:].broadcast_to([P, P]), float(n), None,
                   op0=ALU.is_equal)
            op("ts", ident[:], scrA[:, 0:P], iop[:, 0:1], None, op0=ALU.is_equal,
               inckey=("selinit",))
            op("ms", onescf[:, 1:P], 0.0)
            op("ms", onescf[:, 0:1], 1.0)
            op("ms", onescb[:, 1:P], 0.0)
            op("ms", onescb[:, 0:1], 1.0)
            op("ms", onesrf[:], 1.0)
            for _i, _v in enumerate((EPS, 1.0, -0.5, 0.0)):
                op("ms", consts[:, _i:_i + 1], _v,
                   inckey=("cinit",) if _i == 3 else None)
            op("ms", carry[:], 0.0)
            for ec in range(ECN):
                op("ms", poolv(ec, 0, 3), 0.0, inckey=("minit", ec) if ec == ECN - 1 else None)
            Pr.alias(("ones",), ("minit", ECN - 1))
            op("stt", smc("bsum", 0, FC), smc("b2e0", 0, FC), 1.0, smc("b2e1", 0, FC),
               op0=ALU.mult, op1=ALU.add)
            op("stt", smc("bsum", 0, FC), smc("bsum", 0, FC), 1.0, smc("sb2", 0, FC),
               op0=ALU.mult, op1=ALU.add, inckey=("bsum",))
            wait(("acolsexp",))
            op("ts", acols[:], acols[:], -1.0, None, op0=ALU.mult,
               inckey=("acolsneg",))
            for i in range(NPREP):
                wait(("prepin", i))
                if i >= 1:
                    wait(("prepout", i - 1))
                op("cp", xnb[:, 0:SL], xbuf[:, 0:SL], inckey=("prepcast", i))

            for c in range(nch):
                for ec in range(ECN):
                    if c >= 1:
                        wait(("peg", c - 1, "sw2h1", 7))
                        op("cp", poolv(ec, 0, 3), poolv(ec, 512, 515),
                           inckey=("tail", c, ec))
                    else:
                        Pr.alias(("tail", c, ec), ("minit", ECN - 1))
                if c == 0:
                    wait(("prepout", NPREP - 1))
                wait(("peSx", c))
                wait(("peSx2", c))
                if c > 0:
                    wait(("pebc", c - 1, 4))
                    wait(("pebc", c - 1, 5))
                op("ts", rowv(0), psA[0:1, :], INV_D, None, op0=ALU.mult)
                op("ts", rowv(1), psB[0:1, :], INV_D, None, op0=ALU.mult)
                op("stt", rowv(2), rowv(0), 1.0, rowv(0), op0=ALU.mult, op1=ALU.mult)
                op("stt", rowv(1), rowv(1), 1.0, rowv(2), op0=ALU.mult,
                   op1=ALU.subtract, inckey=("var1", c))
                wait(("rstd1", c))
                op("stt", rowv(0), rowv(0), 1.0, rowv(1), op0=ALU.mult, op1=ALU.mult,
                   inckey=("mr1", c))
                wait(("pebc", c, 0))
                op("cp", repv(0), psY[0][:], inckey=("psyfree", c, 0))
                wait(("pebc", c, 1))
                op("cp", repv(1), psY[1][:], inckey=("psyfree", c, 1))
                if c >= 1:
                    wait(("peg", c - 1, "sw1", 31))
                for fc in range(FC):
                    op("stt", scrYv(fc % 2), xv(fc), 1.0, repv(0), op0=ALU.mult,
                       op1=ALU.mult)
                    op("stt", xnv(fc), scrYv(fc % 2), 1.0, repv(1), op0=ALU.mult,
                       op1=ALU.subtract, inckey=("xnfc", c, fc))
                Pr.alias(("xn", c), ("xnfc", c, FC - 1))
                wait(("peSxn", c))
                wait(("peSxn2", c))
                op("ts", rowv(0), psA[0:1, :], INV_D, None, op0=ALU.mult)
                op("ts", rowv(1), psB[0:1, :], INV_D, None, op0=ALU.mult)
                op("stt", rowv(2), rowv(0), 1.0, rowv(0), op0=ALU.mult, op1=ALU.mult)
                op("stt", rowv(1), rowv(1), 1.0, rowv(2), op0=ALU.mult,
                   op1=ALU.subtract, inckey=("var2", c))
                wait(("rstd2", c))
                op("stt", rowv(0), rowv(0), 1.0, rowv(1), op0=ALU.mult, op1=ALU.mult,
                   inckey=("mr2", c))
                wait(("pebc", c, 2))
                op("cp", repv(0), psY[0][:], inckey=("psyfree", c, 2))
                wait(("pebc", c, 3))
                op("cp", repv(1), psY[1][:], inckey=("psyfree", c, 3))
                if c >= 1:
                    wait(("peg", c - 1, "w1e1", 31))
                for fc in range(FC):
                    op("stt", scrYv(fc % 2), xnv(fc), 1.0, repv(0), op0=ALU.mult,
                       op1=ALU.mult)
                    op("stt", scrYv(fc % 2), scrYv(fc % 2), 1.0, repv(1), op0=ALU.mult,
                       op1=ALU.subtract)
                    op("ts", xlnv(fc), scrYv(fc % 2), smc("wnorm", fc), smc("bnorm", fc),
                       op0=ALU.mult, op1=ALU.add,
                       inckey=("xln", c) if fc == FC - 1 else None)
                if c >= 1:
                    wait(("gsilu", c - 1))
                    wait(("xre", c - 1))
                    for fc in range(FC):
                        op("stt", scrYv(fc % 2), sv(fc), 1.0, gv(fc), op0=ALU.mult,
                           op1=ALU.mult)
                        op("stt", xv(fc), scrYv(fc % 2), 1.0, xv(fc), op0=ALU.mult,
                           op1=ALU.add,
                           inckey=("combine", c - 1) if fc == FC - 1 else None)
                for ec in range(ECN):
                    wait(("xpev", c, ec))
                    wait(("peSx2", c))
                    if ec >= 2:
                        wait(("xc", c, ec - 2))
                    op("ts", scrAv(ec % 2), poolv(ec, 0, TC), smc("w4", ec * 4 + 0),
                       smc("bconv", ec), op0=ALU.mult, op1=ALU.add)
                    for k in range(1, 4):
                        op("stt", scrAv(ec % 2), poolv(ec, k, TC + k),
                           smc("w4", ec * 4 + k), scrAv(ec % 2),
                           op0=ALU.mult, op1=ALU.add,
                           inckey=("conv", c, ec) if k == 3 else None)
                wait(("peg", c, "gate", 0))
                wait(("peg", c, "gate", 1))
                op("cp", rowv(0), psA[0:1, :], inckey=("gatefree0", c))
                op("cp", rowv(1), psB[0:1, :], inckey=("gatefree1", c))
                op("stt", rowv(2), rowv(0), 1.0, rowv(1), op0=ALU.mult,
                   op1=ALU.subtract)
                op("ts", rowv(2), rowv(2), 0.0, None, op0=ALU.is_ge,
                   inckey=("mask0", c))
                op("ts", rowv(0), rowv(2), -1.0, 1.0, op0=ALU.mult, op1=ALU.add,
                   inckey=("mask1", c))
                wait(("pebc", c, 4))
                op("cp", maskv(0), psY[0][:], inckey=("psyfree", c, 36))
                wait(("pebc", c, 5))
                op("cp", maskv(1), psY[1][:], inckey=("psyfree", c, 37))
                wait(("peg", c, "w_in", 31))
                for fc in range(FC):
                    op("stt", xlnv(fc), xnv(fc), 1.0, maskv(0), op0=ALU.mult,
                       op1=ALU.mult, inckey=("mx0", c) if fc == FC - 1 else None)

                tasks = []
                for stage, oc in PE_GROUPS:
                    if stage in ("w2e0h1", "w2e1h0", "w2e1h1", "sw2h0", "sw2h1"):
                        tasks.append((GPOS[(stage, oc)], stage, oc))
                ti = [0]

                def drain_tasks(upto_pos):
                    while ti[0] < len(tasks) and tasks[ti[0]][0] <= upto_pos:
                        _, stage, oc = tasks[ti[0]]
                        r = G_ROT[(stage, oc)]
                        wait(("peg", c, stage, oc))
                        wait(("bankfree", c, G_ROT[("w2e0h0", oc)]))
                        op("stt", gv(oc), gv(oc), 1.0, psM[R2[(stage, oc)] % 4][:],
                           op0=ALU.mult, op1=ALU.add, inckey=("bankfree", c, r))
                        ti[0] += 1

                def emit_carry(sec, ng):
                    slot = (sec * 4 + ng) % 2
                    op("cp", carry[:, sec * NST + ng * 4: sec * NST + ng * 4 + 4],
                       h4v(slot).rearrange("p (n t) -> p n t", n=4)[:, :, TC - 1],
                       inckey=("h4", c, sec, ng))

                def emit_scans(sec):
                    for ng in range(4):
                        g = sec * 4 + ng
                        gg = c * 4 * ECN + g
                        slot = g % 2
                        if gg >= 2:
                            wait(q4key(gg - 2))
                        wait(("b4", c, sec, ng))
                        for j in range(4):
                            n = ng * 4 + j
                            gn = sec * NST + n
                            wait(("a", c, sec, n))
                            op("scan", h4v(slot)[:, j * TC:(j + 1) * TC],
                               arv(gn % 8), b4v(slot)[:, j * TC:(j + 1) * TC],
                               carry[:, sec * NST + n: sec * NST + n + 1],
                               op0=ALU.mult, op1=ALU.add)
                        if ng >= 1:
                            emit_carry(sec, ng - 1)
                        if ng == 1 and hasattr(emit_scans, "post_g1"):
                            emit_scans.post_g1()
                    emit_carry(sec, 3)

                for ec in range(ECN):
                    drain_tasks(WD_POS[ec])
                    r = G_ROT[("w_delta", ec)]
                    wait(("peg", c, "w_delta", ec))
                    gec = c * ECN + ec
                    if gec >= 1:
                        c2, e2 = divmod(gec - 1, ECN)
                        wait(("spln", c2, e2))
                    if gec >= 3:
                        c2, e2 = divmod(gec - 3, ECN)
                        wait(("a", c2, e2, NST - 1))
                    op("ts", spv(0), (psA if ec % 2 == 0 else psB)[:],
                       smc("bdelta", ec), None,
                       op0=ALU.add, inckey=("bankfree", c, r))
                    op("ts", spv(1), spv(0), 0.0, None, op0=ALU.max)
                    op("stt", spv(2), spv(0), -1.0, spv(0), op0=ALU.mult,
                       op1=ALU.min, inckey=("spna", c, ec))
                    mx1_ec = next(e for e in range(ECN)
                                  if WD_POS[e] > GPOS[("w1e0", 31)])
                    assert WD_POS[mx1_ec] < GPOS[("w1e1", 0)]
                    if ec == mx1_ec:
                        wait(("peg", c, "w1e0", 31))
                        if DBG_TAPS and c == TAPC:
                            wait(("tapmx0",))
                        for fc in range(FC):
                            op("stt", xlnv(fc), xnv(fc), 1.0, maskv(1), op0=ALU.mult,
                               op1=ALU.mult,
                               inckey=("mx1", c) if fc == FC - 1 else None)
                    def sp2u(ec=ec, gec=gec):
                        wait(("spln", c, ec))
                        op("stt", dlv(ec), spv(1), 1.0, spv(2), op0=ALU.mult,
                           op1=ALU.add, inckey=("delta", c, ec))
                        wait(("xc", c, ec))
                        if gec >= 2:
                            c2u, e2u = divmod(gec - 2, ECN)
                            wait(("b4", c2u, e2u, 3))
                        op("stt", uv(ec % 2), dlv(ec), 1.0, xcv(ec), op0=ALU.mult,
                           op1=ALU.mult, inckey=("u", c, ec))
                    if ec >= 1:
                        emit_scans.post_g1 = sp2u
                        emit_scans(ec - 1)
                        del emit_scans.post_g1
                    else:
                        sp2u()
                    if ec >= 2:
                        emit_yevac(Pr, eng, op, wait, c, ec - 2)
                emit_scans(ECN - 1)
                for ec in (ECN - 2, ECN - 1):
                    emit_yevac(Pr, eng, op, wait, c, ec)
                drain_tasks(10 ** 9)
                for oc in range(8):
                    r = G_ROT[("w_out", oc)]
                    wait(("peg", c, "w_out", oc))
                    op("stt", sv(oc), xnv(oc), 1.0,
                       psM[R2[("w_out", oc)] % 4][:], op0=ALU.mult,
                       op1=ALU.add, inckey=("bankfree", c, r))
                    if oc == 7:
                        Pr.alias(("schunk", c), ("bankfree", c, r))
            c = nch
            wait(("gsilu", c - 1))
            wait(("xre", c - 1))
            for fc in range(FC):
                op("stt", scrYv(fc % 2), sv(fc), 1.0, gv(fc), op0=ALU.mult,
                   op1=ALU.mult)
                op("stt", xv(fc), scrYv(fc % 2), 1.0, xv(fc), op0=ALU.mult,
                   op1=ALU.add,
                   inckey=("combine", c - 1) if fc == FC - 1 else None)

        def emit_scalar(Pr, eng):
            wait = lambda k: Pr.wait("scalar", eng, k)
            dry = Pr.dry

            def op(kind, *a, inckey=None, **kw):
                if dry:
                    instr = None
                else:
                    f = {"act": eng.activation, "cp": eng.copy}[kind]
                    instr = f(*a, **kw)
                if inckey is not None:
                    Pr.inc(instr, "se", inckey, 1)
                return instr

            wait(("sm",))
            wait(("cinit",))
            op("act", acols[:], smc("Alog", 0, ECN * NST), AF.Exp,
               inckey=("acolsexp",))

            for c in range(nch):
                wait(("xin", c))
                if c == 0:
                    wait(("selinit",))
                if c > 0:
                    wait(("pebc", c - 1, 4))
                    wait(("pebc", c - 1, 5))
                if c > 0:
                    wait(("peg", c - 1, "w_out", 7))
                    wait(("schunk", c - 1))
                for fc in range(FC):
                    if fc >= 4:
                        wait(("peSx2f", c, fc - 4))
                    op("act", xsqv(fc), xv(fc), AF.Square,
                       inckey=("xsq", c, fc))
                wait(("var1", c))
                op("act", rowv(2), rowv(1), AF.Ln, bias=EPS)
                op("act", rowv(1), rowv(2), AF.Exp, scale=-0.5,
                   inckey=("rstd1", c))
                if c > 0:
                    wait(("peg", c - 1, "w1e1", 31))
                for fc in range(FC):
                    wait(("xnfc", c, fc))
                    if fc >= 4:
                        wait(("peSxn2f", c, fc - 4))
                    op("act", scrQv(fc), xnv(fc), AF.Square,
                       inckey=("xnsq", c, fc))
                wait(("var2", c))
                wait(("pebc", c, 0))
                op("act", rowv(2), rowv(1), AF.Ln, bias=EPS)
                op("act", rowv(1), rowv(2), AF.Exp, scale=-0.5,
                   inckey=("rstd2", c))
                if c >= 1:
                    wait(("w2done", c - 1))
                    wait(("combine2", c - 1)) if False else None
                    for fc in range(FC):
                        op("act", gv(fc), gv(fc), AF.Silu, bias=smc("bsum", fc),
                           inckey=("gsilu", c - 1) if fc == FC - 1 else None)
                for oc in range(32):
                    r = G_ROT[("w_in", oc)]
                    wait(("peg", c, "w_in", oc))
                    if oc < ECN:
                        wait(("tail", c, oc))
                        op("cp", poolv(oc, 3, 515), psM[R2[("w_in", oc)] % 4][:],
                           inckey=("bankfree", c, r))
                        Pr.alias(("xpev", c, oc), ("bankfree", c, r))
                    else:
                        if c >= 1:
                            wait(("peg", c - 1, "w_out", 7))
                        op("act", szv(oc - ECN), psM[R2[("w_in", oc)] % 4][:],
                           AF.Silu, inckey=("bankfree", c, r))
                for ec in range(ECN):
                    wait(("conv", c, ec))
                    if c >= 1:
                        wait(("peg", c - 1, "w_delta", 15))
                        wait(("yg", c - 1, ec))
                    op("act", xcv(ec), scrAv(ec % 2), AF.Silu,
                       inckey=("xc", c, ec))
                wait(("peg", c, "w_b", 0))
                if c >= 1:
                    wait(("pebr", c - 1, NST - 1))
                op("cp", bpad[:], psA[:], inckey=("bpadev", c))
                wait(("peg", c, "w_c", 0))
                if c >= 1:
                    wait(("pecr", c - 1, NST - 1))
                op("cp", cpad[:], psB[:], inckey=("cpadev", c))
                for n in range(NST):
                    wait(("pebr", c, n))
                    if c >= 1:
                        wait(("b4", c - 1, ECN - 1, n // 4))
                    op("cp", brv(n), psY[n % 2][:], inckey=("brevac", c, n))
                    Pr.alias(("psyfree", c, 4 + n), ("brevac", c, n))
                for n in range(NST):
                    wait(("pecr", c, n))
                    if c >= 1:
                        wait(("q4", c - 1, ECN - 1, n // 4))
                    op("cp", crv(n), psY[n % 2][:], inckey=("crevac", c, n))
                    Pr.alias(("psyfree", c, 20 + n), ("crevac", c, n))
                pend_a = []

                def emit_a_group(ec, ng):
                    wait(("delta", c, ec))
                    wait(("acolsneg",))
                    for n in range(ng * 4, ng * 4 + 4):
                        gan = (c * ECN + ec) * NST + n
                        if gan >= 8:
                            prevg = (gan - 8) // 4
                            wait(sgkey(prevg))
                        op("act", arv((ec * NST + n) % 8), dlv(ec), AF.Exp,
                           scale=acols[:, ec * NST + n: ec * NST + n + 1],
                           inckey=("a", c, ec, n))

                def drain_a(k=1):
                    while k > 0 and pend_a:
                        emit_a_group(*pend_a.pop(0))
                        k -= 1

                def emit_a(ec):
                    for ng in range(4):
                        emit_a_group(ec, ng)

                for stage, oc in PE_GROUPS:
                    if stage == "w_delta":
                        ec = oc
                        wait(("spna", c, ec))
                        op("act", spv(0), spv(2), AF.Exp)
                        op("act", spv(2), spv(0), AF.Ln, bias=1.0,
                           inckey=("spln", c, ec))
                        if ec >= 1:
                            emit_a(ec - 1)
                    elif stage in ("w1e0", "w1e1", "sw1"):
                        drain_a(1)
                        kc = oc % 16
                        half = oc // 16
                        r = G_ROT[(stage, oc)]
                        wait(("peg", c, stage, oc))
                        if stage == "w1e0" and half == 0:
                            wait(("conv", c, kc))
                        else:
                            wait(("peg", c, H1_PREV[(stage, half)], 7))
                        bname = {"w1e0": "b1e0", "w1e1": "b1e1", "sw1": "sb1"}[stage]
                        path = {"w1e0": "e0", "w1e1": "e1", "sw1": "sh"}[stage]
                        if DBG_TAPS and c == TAPC and (stage, half) in H1_PREV:
                            wait(("tapdma", (stage, half), kc))
                        op("act", poolv(kc, 0, TC), psM[R2[(stage, oc)] % 4][:],
                           AF.Relu, bias=smc(bname, half * 16 + kc),
                           inckey=("bankfree", c, r))
                        Pr.alias(("h1", c, W2_OF[path] + f"h{half}", kc),
                                 ("bankfree", c, r))
                    elif stage == "w2e0h0":
                        drain_a(1)
                        r = G_ROT[(stage, oc)]
                        wait(("peg", c, stage, oc))
                        if c >= 1 and oc == 0:
                            wait(("combine", c - 1))
                        op("cp", gv(oc), psM[R2[(stage, oc)] % 4][:],
                           inckey=("bankfree", c, r))
                drain_a(99)
                emit_a(ECN - 1)
                Pr.alias(("w2done", c), ("bankfree", c, G_ROT[("sw2h1", 7)]))
            c = nch
            wait(("w2done", c - 1))
            for fc in range(FC):
                op("act", gv(fc), gv(fc), AF.Silu, bias=smc("bsum", fc),
                   inckey=("gsilu", c - 1) if fc == FC - 1 else None)

        def emit_all(Pr, engs):
            emit_sync(Pr, engs.get("sync"))
            emit_gpsimd(Pr, engs.get("gpsimd"))
            emit_tensor(Pr, engs.get("tensor"))
            emit_vector(Pr, engs.get("vector"))
            emit_scalar(Pr, engs.get("scalar"))

        dryP = Prog(dry=True)
        emit_all(dryP, {})
        realP = Prog(dry=False, events=dryP.events, sems=sems)

        @block.sync
        def _(sync):
            emit_sync(realP, sync)

        @block.gpsimd
        def _(g):
            emit_gpsimd(realP, g)

        @block.tensor
        def _(t):
            emit_tensor(realP, t)

        @block.vector
        def _(v):
            emit_vector(realP, v)

        @block.scalar
        def _(s):
            emit_scalar(realP, s)

        for k in ("sy", "pe", "ve", "se", "po", "pd"):
            assert dryP.cnt[k] == realP.cnt[k], (k, dryP.cnt[k], realP.cnt[k])
    return nc


def kernel(**inputs):
    x = np.asarray(inputs["L0_attn_x"])
    wt = host_pack_wtiles(inputs)
    sm = host_pack_smalls(inputs)
    nch = x.shape[1] // TC
    nc = build_nc(nch)
    in_maps = []
    for b in range(x.shape[0]):
        in_maps.append({
            "xT": np.ascontiguousarray(x[b].T),
            "wt": wt,
            "smI": sm,
        })
    res = run_bass_kernel_spmd(nc, in_maps, core_ids=list(range(x.shape[0])))
    out = np.stack([np.asarray(res.results[b]["outT"]).T for b in range(x.shape[0])], axis=0)
    return out


# version 07
# speedup vs baseline: 1.0030x; 1.0030x over previous
import contextlib
import numpy as np

import concourse.bass as bass
import concourse.mybir as mybir
from concourse.bass_utils import run_bass_kernel_spmd

F32 = mybir.dt.float32
BF16 = mybir.dt.bfloat16
AF = mybir.ActivationFunctionType
ALU = mybir.AluOpType

D, E, HID, NST = 1024, 2048, 4096, 16
DBG_TAPS = False
DBG2 = False
TAPC = 1
FC, ECN, HCN = 8, 16, 32
TC = 512
P = 128
NCH_FULL = 16
EPS = 1e-5
INV_D = 1.0 / D

GROUP_K = {
    "w_in": 8, "w_b": 16, "w_c": 16, "gate": 8, "w_delta": 16,
    "w1e0": 8, "w1e1": 8, "sw1": 8,
    "w2e0h0": 16, "w2e0h1": 16, "w2e1h0": 16, "w2e1h1": 16,
    "sw2h0": 16, "sw2h1": 16, "w_out": 16,
}
MAIN_STAGES = {"w_in", "w_delta", "w1e0", "w1e1", "sw1", "w2e0h0", "w2e0h1",
               "w2e1h0", "w2e1h1", "sw2h0", "sw2h1", "w_out"}
W1_OF = {"e0": "w1e0", "e1": "w1e1", "sh": "sw1"}
W2_OF = {"e0": "w2e0", "e1": "w2e1", "sh": "sw2"}
H1_PREV = {("w1e0", 1): "w2e0h0", ("w1e1", 0): "w2e0h1", ("w1e1", 1): "w2e1h0",
           ("sw1", 0): "w2e1h1", ("sw1", 1): "sw2h0"}


def pe_group_sequence():
    seq = [("w_in", oc) for oc in range(32)]
    seq += [("w_b", 0), ("w_c", 0), ("gate", 0), ("gate", 1)]
    ffn = []
    for path in ("e0", "e1", "sh"):
        for half in range(2):
            ffn += [(W1_OF[path], oc) for oc in range(16 * half, 16 * half + 16)]
            ffn += [(W2_OF[path] + f"h{half}", oc) for oc in range(8)]
    inter = [("w_delta", 0), ("w_delta", 1), ("w_delta", 2)]
    nxt = 3
    total_t = sum(GROUP_K[g[0]] for g in ffn)
    cum = 0
    for i, g in enumerate(ffn):
        inter.append(g)
        cum += GROUP_K[g[0]]
        if nxt < ECN and cum % 16 == 0 and cum >= (nxt - 2) * total_t / 13.0:
            inter.append(("w_delta", nxt))
            nxt += 1
    assert nxt == ECN, nxt
    seq += inter
    seq += [("w_out", oc) for oc in range(8)]
    return seq


PE_GROUPS = pe_group_sequence()
MAIN_GROUPS = [g for g in PE_GROUPS if g[0] in MAIN_STAGES]
NMAIN = len(MAIN_GROUPS)
G_ROT = {g: i for i, g in enumerate(MAIN_GROUPS)}
NONWD = [g for g in MAIN_GROUPS if g[0] != "w_delta"]
R2 = {g: i for i, g in enumerate(NONWD)}
NR2 = len(NONWD)
GPOS = {g: i for i, g in enumerate(PE_GROUPS)}
WD_POS = {oc: GPOS[("w_delta", oc)] for oc in range(ECN)}
FFN_STAGES = {"w1e0", "w1e1", "sw1", "w2e0h0", "w2e0h1", "w2e1h0", "w2e1h1",
              "sw2h0", "sw2h1"}


def wtile_order():
    L = []
    for stage, oc in PE_GROUPS:
        for kc in range(GROUP_K[stage]):
            L.append((stage, oc, kc))
    assert len(L) % 16 == 0
    return L


WTILES = wtile_order()
NT = len(WTILES)
NSLOT = NT // 16
RING = 2
TEN_Y0 = 38
SL = 2048
NPREP = NT * P // SL
assert NPREP * SL == NT * P


def host_pack_wtiles(inp):
    ws = {
        "w_in": inp["L0_attn_w_in"], "w_b": inp["L0_attn_w_b"],
        "w_c": inp["L0_attn_w_c"], "w_delta": inp["L0_attn_w_delta"],
        "w_out": inp["L0_attn_w_out"], "gate_w": inp["L0_ffn_gate_w"],
        "w1": inp["L0_ffn_w1"], "w2": inp["L0_ffn_w2"],
        "sw1": inp["L0_ffn_sw1"], "sw2": inp["L0_ffn_sw2"],
    }
    ws = {k: np.asarray(v) for k, v in ws.items()}
    arr = np.zeros((NT, P, P), dtype=np.float32)
    for i, (stage, oc, kc) in enumerate(WTILES):
        if stage == "w_in":
            arr[i] = ws["w_in"][kc * P:(kc + 1) * P, oc * P:(oc + 1) * P]
        elif stage == "w_b":
            arr[i, :, 0:NST] = ws["w_b"][kc * P:(kc + 1) * P, :]
        elif stage == "w_c":
            arr[i, :, 0:NST] = ws["w_c"][kc * P:(kc + 1) * P, :]
        elif stage == "gate":
            arr[i, :, 0] = ws["gate_w"][oc, kc * P:(kc + 1) * P]
        elif stage == "w_delta":
            arr[i] = ws["w_delta"][kc * P:(kc + 1) * P, oc * P:(oc + 1) * P]
        elif stage in ("w1e0", "w1e1", "sw1"):
            w = {"w1e0": ws["w1"][0], "w1e1": ws["w1"][1], "sw1": ws["sw1"]}[stage]
            arr[i] = w[kc * P:(kc + 1) * P, oc * P:(oc + 1) * P]
        elif stage[:-2] in ("w2e0", "w2e1", "sw2"):
            w = {"w2e0": ws["w2"][0], "w2e1": ws["w2"][1], "sw2": ws["sw2"]}[stage[:-2]]
            kcg = int(stage[-1]) * 16 + kc
            arr[i] = w[kcg * P:(kcg + 1) * P, oc * P:(oc + 1) * P]
        elif stage == "w_out":
            arr[i] = ws["w_out"][kc * P:(kc + 1) * P, oc * P:(oc + 1) * P]
    return np.ascontiguousarray(arr.transpose(1, 0, 2).reshape(P, NT * P))


SM = {}
_off = 0
for _n, _w in [("wnorm", FC), ("bnorm", FC), ("w4", ECN * 4), ("bconv", ECN),
               ("bdelta", ECN), ("Dcol", ECN), ("Alog", ECN * NST),
               ("b1e0", HCN), ("b1e1", HCN), ("sb1", HCN),
               ("b2e0", FC), ("b2e1", FC), ("sb2", FC), ("bsum", FC)]:
    SM[_n] = (_off, _off + _w)
    _off += _w
SMW = _off


def host_pack_smalls(inp):
    sm = np.zeros((P, SMW), dtype=np.float32)

    def put(name, vec, w):
        a, b = SM[name]
        sm[:, a:b] = np.asarray(vec).reshape(w, P).T

    put("wnorm", inp["L0_attn_w_norm"], FC)
    put("bnorm", inp["L0_attn_b_norm"], FC)
    a, b = SM["w4"]
    sm[:, a:b] = np.asarray(inp["L0_attn_w_conv"]).reshape(ECN, P, 4).transpose(1, 0, 2).reshape(P, ECN * 4)
    put("bconv", inp["L0_attn_b_conv"], ECN)
    put("bdelta", inp["L0_attn_b_delta"], ECN)
    put("Dcol", inp["L0_attn_D_param"], ECN)
    a, b = SM["Alog"]
    sm[:, a:b] = np.asarray(inp["L0_attn_A_log"]).reshape(ECN, P, NST).transpose(1, 0, 2).reshape(P, ECN * NST)
    put("b1e0", inp["L0_ffn_b1"][0], HCN)
    put("b1e1", inp["L0_ffn_b1"][1], HCN)
    put("sb1", inp["L0_ffn_sb1"], HCN)
    put("b2e0", inp["L0_ffn_b2"][0], FC)
    put("b2e1", inp["L0_ffn_b2"][1], FC)
    put("sb2", inp["L0_ffn_sb2"], FC)
    return sm


class Prog:
    def __init__(self, dry, events=None, sems=None):
        self.dry = dry
        self.cnt = {k: 0 for k in ("sy", "pe", "ve", "se", "po", "pd")}
        self.events = events if events is not None else {}
        self.sems = sems or {}
        self.waited = {}

    def inc(self, instr, sem, key, n=1):
        self.cnt[sem] += n
        ev = (sem, self.cnt[sem])
        if self.dry:
            assert key not in self.events, f"dup event {key}"
            self.events[key] = ev
        else:
            assert self.events.get(key) == ev, (key, self.events.get(key), ev)
            if instr is not None:
                instr.then_inc(self.sems[sem], n)

    def alias(self, newkey, oldkey):
        if self.dry:
            self.events[newkey] = self.events[oldkey]

    def wait(self, engname, eng, key):
        if self.dry:
            return
        sem, val = self.events[key]
        w = self.waited.setdefault(engname, {})
        if w.get(sem, -1) >= val:
            return
        w[sem] = val
        eng.wait_ge(self.sems[sem], val)


def build_nc(nch=NCH_FULL):
    T = nch * TC
    nc = bass.Bass("TRN2", target_bir_lowering=False)

    xT = nc.dram_tensor("xT", [D, T], F32, kind="ExternalInput").ap()
    wt = nc.dram_tensor("wt", [P, NT * P], F32, kind="ExternalInput").ap()
    smI = nc.dram_tensor("smI", [P, SMW], F32, kind="ExternalInput").ap()
    outT = nc.dram_tensor("outT", [D, T], F32, kind="ExternalOutput").ap()
    dbg = nc.dram_tensor("dbg", [P, 8 * 8192], BF16, kind="ExternalOutput").ap() if DBG_TAPS else None
    dbg2 = nc.dram_tensor("dbg2", [P, 4096], F32, kind="ExternalOutput").ap() if DBG2 else None
    wsb = nc.dram_tensor("wsb", [P, NT * P], BF16, kind="Internal").ap()

    ctx = contextlib.ExitStack()
    with ctx:
        sb = lambda name, shape, dt: ctx.enter_context(nc.sbuf_tensor(name, shape, dt))
        wring = sb("wring", [P, RING * 16 * P], BF16)
        xbuf = sb("xbuf", [P, FC * TC], F32)
        xnb = sb("xnb", [P, FC * TC], BF16)
        xlnmx = sb("xlnmx", [P, FC * TC], BF16)
        pool = sb("pool", [P, ECN * 516], BF16)
        szb = sb("szb", [P, ECN * TC], BF16)
        xcyg = sb("xcyg", [P, ECN * TC], BF16)
        deltar = sb("deltar", [P, 3 * TC], BF16)
        brep = sb("brep", [P, NST * TC], BF16)
        crep = sb("crep", [P, NST * TC], BF16)
        bpad = sb("bpad", [P, TC], BF16)
        cpad = sb("cpad", [P, TC], BF16)
        aring = sb("aring", [P, 8 * TC], BF16)
        b4r = sb("b4r", [P, 2 * 4 * TC], BF16)
        h4r = sb("h4r", [P, 4 * 4 * TC], BF16)
        ur = sb("ur", [P, 2 * TC], BF16)
        gacc = sb("gacc", [P, FC * TC], BF16)
        sres = sb("sres", [P, FC * TC], BF16)
        carry = sb("carry", [P, ECN * NST], F32)
        smalls = sb("smalls", [P, SMW], F32)
        acols = sb("acols", [P, ECN * NST], F32)
        selt = sb("selt", [P, NST * P], BF16)
        ident = sb("ident", [P, P], BF16)
        onescf = sb("onescf", [P, P], F32)
        onescb = sb("onescb", [P, P], BF16)
        onesrf = sb("onesrf", [1, P], F32)
        maskr = sb("maskr", [P, 2 * TC], BF16)
        repf = sb("repf", [P, 2 * TC], F32)
        scrA = sb("scrA", [P, 2 * TC], F32)
        scrSP = sb("scrSP", [P, 3 * TC], BF16)
        scrY = sb("scrY", [P, TC], F32)
        rows = sb("rows", [1, 3 * TC], F32)
        iop = sb("iop", [P, 1], F32)
        consts = sb("consts", [P, 4], F32)

        psM = [ctx.enter_context(nc.psum_tensor(f"psM{i}", [P, TC], F32)) for i in range(4)]
        psY = [ctx.enter_context(nc.psum_tensor(f"psY{i}", [P, TC], F32)) for i in range(2)]
        psA = ctx.enter_context(nc.psum_tensor("psA", [P, TC], F32))
        psB = ctx.enter_context(nc.psum_tensor("psB", [P, TC], F32))

        sems = {k: ctx.enter_context(nc.semaphore(k))
                for k in ("sy", "pe", "ve", "se", "po", "pd")}
        for _i, _v in enumerate((EPS, 1.0, -0.5, 0.0)):
            nc.const_aps.register(F32, _v, consts[:, _i:_i + 1])
        block = ctx.enter_context(nc.Block())

        xv = lambda fc: xbuf[:, fc * TC:(fc + 1) * TC]
        xnv = lambda fc: xnb[:, fc * TC:(fc + 1) * TC]
        xlnv = lambda fc: xlnmx[:, fc * TC:(fc + 1) * TC]
        szv = lambda ec: szb[:, ec * TC:(ec + 1) * TC]
        xcv = lambda ec: xcyg[:, ec * TC:(ec + 1) * TC]
        gv = lambda fc: gacc[:, fc * TC:(fc + 1) * TC]
        sv = lambda fc: sres[:, fc * TC:(fc + 1) * TC]
        poolv = lambda i, c0, c1: pool[:, i * 516 + c0: i * 516 + c1]
        brv = lambda n: brep[:, n * TC:(n + 1) * TC]
        crv = lambda n: crep[:, n * TC:(n + 1) * TC]
        arv = lambda s: aring[:, s * TC:(s + 1) * TC]
        b4v = lambda s: b4r[:, s * 4 * TC:(s + 1) * 4 * TC]
        h4v = lambda s: h4r[:, s * 4 * TC:(s + 1) * 4 * TC]
        uv = lambda s: ur[:, s * TC:(s + 1) * TC]
        scrSPb = scrA[:].bitcast(BF16)
        spv = lambda j: scrSPb[:, j * TC:(j + 1) * TC]
        dlv = lambda ec: deltar[:, (ec % 3) * TC:(ec % 3 + 1) * TC]
        scrAv = lambda s: scrA[:, s * TC:(s + 1) * TC]
        scrYv = lambda s: scrY[:, 0:TC]
        xsqv = lambda s: xnb[:].bitcast(F32)[:, (s % 4) * TC:(s % 4 + 1) * TC]
        scrQv = lambda s: xlnmx[:, (s % 4) * TC:(s % 4 + 1) * TC]
        rowv = lambda i: rows[0:1, i * TC:(i + 1) * TC]
        maskv = lambda i: maskr[:, i * TC:(i + 1) * TC]
        repv = lambda i: repf[:, i * TC:(i + 1) * TC]
        selv = lambda n: selt[:, n * P:(n + 1) * P]

        def smc(name, j=0, w=1):
            a, b = SM[name]
            assert a + j + w <= b
            return smalls[:, a + j:a + j + w]

        def ring_tile(t):
            tt = t % NT
            s = (t // 16) % RING
            return wring[:, (s * 16 + tt % 16) * P:(s * 16 + tt % 16 + 1) * P]

        TAP_NXT = {("w1e0", 0): ("w1e0", 1), ("w1e0", 1): ("w1e1", 0),
                   ("w1e1", 0): ("w1e1", 1), ("w1e1", 1): ("sw1", 0),
                   ("sw1", 0): ("sw1", 1), ("sw1", 1): None}
        TAP_SLOT = {}
        if DBG_TAPS:
            last_tile = {}
            for i, (st_, oc_, kc_) in enumerate(WTILES):
                last_tile[(st_, oc_)] = i
            for pi, (st_, half) in enumerate([("w1e0", 0), ("w1e0", 1), ("w1e1", 0),
                                              ("w1e1", 1), ("sw1", 0), ("sw1", 1)]):
                TAP_SLOT[last_tile[(st_, half * 16 + 15)] // 16] = (pi, st_, half)
        xTr = xT.rearrange("(f p) t -> p f t", p=P)
        outTr = outT.rearrange("(f p) t -> p f t", p=P)

        def sgkey(gg):
            c2, r = divmod(gg, 4 * ECN)
            return ("h4", c2, r // 4, r % 4)

        def q4key(gg):
            c2, r = divmod(gg, 4 * ECN)
            return ("q4", c2, r // 4, r % 4)

        def peygkey(gg):
            c2, r = divmod(gg, 4 * ECN)
            return ("peyg", c2, r // 4, r % 4)

        def akey(gan):
            c2, r = divmod(gan, NST * ECN)
            return ("a", c2, r // NST, r % NST)

        def emit_sync(Pr, eng):
            wait = lambda k: Pr.wait("sync", eng, k)

            def dma(dst, src, key):
                instr = None if Pr.dry else eng.dma_start(out=dst, in_=src)
                Pr.inc(instr, "sy", key, 16)

            dma(smalls[:], smI, ("sm",))
            for i in range(NPREP):
                if i > 0:
                    wait(("prepcast", i - 1))
                dma(xbuf[:, 0:SL], wt[:, i * SL:(i + 1) * SL], ("prepin", i))
            for c in range(nch):
                if c == 0:
                    wait(("prepout", NPREP - 1))
                dma(xbuf[:], xTr[:, :, c * TC:(c + 1) * TC], ("xin", c))
                for s in range(NSLOT):
                    g = c * NSLOT + s
                    if g >= RING:
                        wait(("wfree_g", g - RING))
                    dma(wring[:, (g % RING) * 16 * P:(g % RING + 1) * 16 * P],
                        wsb[:, s * 16 * P:(s + 1) * 16 * P], ("wslot", c, s))
                    if s == 12 and c >= 1:
                        wait(("xn", c))
                        wait(("peSx", c))
                        wait(("xsq", c, FC - 1))
                        dma(xbuf[:], xTr[:, :, (c - 1) * TC:c * TC], ("xre", c - 1))
                        wait(("combine", c - 1))
                        dma(outTr[:, :, (c - 1) * TC:c * TC], xbuf[:], ("outdma", c - 1))
                    if DBG_TAPS and c == TAPC and s in TAP_SLOT:
                        pi, st_, half = TAP_SLOT[s]
                        if (st_, half) == ("mx",):
                            pass
                        for kc in range(16):
                            r = G_ROT[(st_, half * 16 + kc)]
                            wait(("bankfree", TAPC, r))
                            nxt = TAP_NXT[(st_, half)]
                            key = ("tapdma", nxt, kc) if nxt else ("tapend", kc)
                            dma(dbg[:, (pi * 16 + kc) * TC:(pi * 16 + kc + 1) * TC],
                                poolv(kc, 0, TC), key)
                        if pi == 0:
                            wait(("mx0", TAPC))
                            dma(dbg[:, 96 * TC:104 * TC], xlnmx[:], ("tapmx0",))
                        if pi == 2:
                            wait(("mx1", TAPC))
                            dma(dbg[:, 104 * TC:112 * TC], xlnmx[:], ("tapmx1",))

            wait(("xn", nch - 1))
            wait(("peSx", nch - 1))
            wait(("xsq", nch - 1, FC - 1))
            dma(xbuf[:], xTr[:, :, (nch - 1) * TC:nch * TC], ("xre", nch - 1))
            wait(("combine", nch - 1))
            dma(outTr[:, :, (nch - 1) * TC:nch * TC], xbuf[:], ("outdma", nch - 1))
            if not Pr.dry:
                eng.wait_ge(Pr.sems["sy"], Pr.cnt["sy"])

        def emit_gpsimd(Pr, eng):
            wait = lambda k: Pr.wait("gpsimd", eng, k)
            dry = Pr.dry
            if not dry:
                eng.iota(iop[:], [[0, 1]], channel_multiplier=1,
                         allow_small_or_imprecise_dtypes=True)
                i2 = eng.iota(scrA[:, 0:P], [[1, P]], channel_multiplier=0,
                              allow_small_or_imprecise_dtypes=True)
            else:
                i2 = None
            Pr.inc(i2, "po", ("iotas",), 1)
            for i in range(NPREP):
                wait(("prepcast", i))
                instr = None if dry else eng.dma_start(
                    out=wsb[:, i * SL:(i + 1) * SL], in_=xnb[:, 0:SL])
                Pr.inc(instr, "pd", ("prepout", i), 16)
            def tap2(i, w, src_ap, key):
                if not DBG2:
                    return
                wait(w)
                instr = None if dry else eng.dma_start(
                    out=dbg2[:, i * 512:(i + 1) * 512], in_=src_ap)
                Pr.inc(instr, "pd", ("t2", key), 16)

            for c in range(nch):
                if DBG2 and c == 1:
                    tap2(0, ("tail", 1, 0), pool[:, 0:512], "tail")
                    if DBG2:
                        wait(("h4", 0, ECN - 1, 3))
                        instr = None if dry else eng.dma_start(
                            out=dbg2[:, 4 * 512:4 * 512 + 256], in_=carry[:, 0:256])
                        Pr.inc(instr, "pd", ("t2", "carry"), 16)
                    tap2(1, ("bankfree", 1, G_ROT[("w_in", 0)]), pool[:, 0:512], "xpev")
                    tap2(2, ("conv", 1, 0), scrA[:, 0:512], "cv")
                    tap2(3, ("xc", 1, 0), xcyg[:, 0:512], "xc")
                    tap2(4, ("h4", 0, ECN - 1, 3), carry[:, 0:256], "carry") if False else None
                    tap2(5, ("delta", 1, 0), deltar[:, 0:512], "delta")
                    tap2(6, ("a", 1, 0, 3), aring[:, 0:512], "a0")

                def po_q4(ec, ng):
                    wait(("h4", c, ec, ng))
                    for n in range(ng * 4, ng * 4 + 4):
                        wait(("crevac", c, n))
                    instr = None if dry else eng.tensor_mul(
                        h4v(ng).rearrange("p (g t) -> p g t", g=4),
                        crep[:, ng * 4 * TC:(ng + 1) * 4 * TC].rearrange(
                            "p (g t) -> p g t", g=4),
                        h4v(ng).rearrange("p (g t) -> p g t", g=4))
                    Pr.inc(instr, "po", ("q4", c, ec, ng), 1)

                mx1_ec = next(e for e in range(ECN)
                              if WD_POS[e] > GPOS[("w1e0", 31)])
                assert WD_POS[mx1_ec] < GPOS[("w1e1", 0)]
                wait(("psyfree", c, 36))
                wait(("peg", c, "w_in", 31))
                for fc in range(FC):
                    instr = None if dry else eng.tensor_mul(
                        xlnv(fc), xnv(fc), maskv(0))
                    if fc == FC - 1:
                        Pr.inc(instr, "po", ("mx0", c), 1)
                for ec in range(ECN):
                    if ec == mx1_ec - 1:
                        wait(("psyfree", c, 37))
                        wait(("peg", c, "w1e0", 31))
                        for fc in range(FC):
                            instr = None if dry else eng.tensor_mul(
                                xlnv(fc), xnv(fc), maskv(1))
                            if fc == FC - 1:
                                Pr.inc(instr, "po", ("mx1", c), 1)
                    for ng in range(4):
                        po_q4(ec, ng)

        def emit_tensor(Pr, eng):
            wait = lambda k: Pr.wait("tensor", eng, k)
            dry = Pr.dry

            def mm(ps, lhsT, rhs, start, stop, inckey=None):
                instr = None if dry else eng.matmul(ps, lhsT, rhs, start=start, stop=stop)
                if inckey is not None:
                    Pr.inc(instr, "pe", inckey, 1)
                return instr

            tcur = [0]

            def take_tile(c):
                t = tcur[0]
                tcur[0] += 1
                if t % 16 == 0:
                    wait(("wslot", c, (t % NT) // 16))
                return ring_tile(t)

            wait(("ones",))
            wait(("selinit",))
            for c in range(nch):
                assert tcur[0] == c * NT
                wait(("xin", c))
                if c > 0:
                    wait(("bankfree", c - 1, G_ROT[("w_delta", ECN - 2)]))
                    wait(("bankfree", c - 1, G_ROT[("w_delta", ECN - 1)]))
                for fc in range(FC):
                    mm(psA[:], onescf[:], xv(fc), fc == 0, fc == FC - 1,
                       ("peSx", c) if fc == FC - 1 else None)
                for fc in range(FC):
                    wait(("xsq", c, fc))
                    mm(psB[:], onescf[:], xsqv(fc), fc == 0, fc == FC - 1,
                       ("peSx2f", c, fc))
                Pr.alias(("peSx2", c), ("peSx2f", c, FC - 1))
                wait(("rstd1", c))
                if c > 0:
                    wait(("psyfree", c - 1, TEN_Y0 + ECN - 2))
                mm(psY[0][:], onesrf[:], rowv(1), True, True, ("pebc", c, 0))
                wait(("mr1", c))
                if c > 0:
                    wait(("psyfree", c - 1, TEN_Y0 + ECN - 1))
                mm(psY[1][:], onesrf[:], rowv(0), True, True, ("pebc", c, 1))
                wait(("var1", c))
                for fc in range(FC):
                    wait(("xnfc", c, fc))
                    mm(psA[:], onescb[:], xnv(fc), fc == 0, fc == FC - 1,
                       ("peSxn", c) if fc == FC - 1 else None)
                for fc in range(FC):
                    wait(("xnsq", c, fc))
                    mm(psB[:], onescb[:], scrQv(fc), fc == 0, fc == FC - 1,
                       ("peSxn2f", c, fc))
                Pr.alias(("peSxn2", c), ("peSxn2f", c, FC - 1))
                wait(("rstd2", c))
                wait(("psyfree", c, 0))
                mm(psY[0][:], onesrf[:], rowv(1), True, True, ("pebc", c, 2))
                wait(("mr2", c))
                wait(("psyfree", c, 1))
                mm(psY[1][:], onesrf[:], rowv(0), True, True, ("pebc", c, 3))

                yg_next = [0]

                def emit_y_ng():
                    g = yg_next[0]
                    if g >= 4 * ECN:
                        return False
                    ec, ng = g // 4, g % 4
                    ten = TEN_Y0 + ec
                    bank = ec % 2
                    if ng == 0:
                        wait(("psyfree", c, ten - 2))
                    wait(("q4", c, ec, ng))
                    slot = g % 4
                    for j in range(4):
                        n = ng * 4 + j
                        key = None
                        if n == NST - 1:
                            key = ("pey", c, ec)
                        elif j == 3:
                            key = ("peyg", c, ec, ng)
                        mm(psY[bank][:], ident[:], h4v(slot)[:, j * TC:(j + 1) * TC],
                           n == 0, n == NST - 1, key)
                    if ng == 3:
                        Pr.alias(("peyg", c, ec, 3), ("pey", c, ec))
                    yg_next[0] += 1
                    return True

                for stage, oc in PE_GROUPS:
                    K = GROUP_K[stage]
                    gkey = ("peg", c, stage, oc)
                    ps = None
                    if stage == "w_delta":
                        r = G_ROT[(stage, oc)]
                        if oc >= 2:
                            wait(("bankfree", c, G_ROT[("w_delta", oc - 2)]))
                        elif oc == 0:
                            wait(("gatefree0", c))
                        else:
                            wait(("gatefree1", c))
                        ps = psA if oc % 2 == 0 else psB
                    elif stage in MAIN_STAGES:
                        r = G_ROT[(stage, oc)]
                        r2 = R2[(stage, oc)]
                        if r2 >= 4:
                            wait(("bankfree", c, G_ROT[NONWD[r2 - 4]]))
                        elif c > 0:
                            wait(("bankfree", c - 1, G_ROT[NONWD[NR2 - 4 + r2]]))
                        ps = psM[r2 % 4]
                    if stage == "w_in":
                        wait(("xln", c))
                        rhs = [xlnv(kc) for kc in range(FC)]
                    elif stage == "w_b":
                        for kc in range(ECN):
                            wait(("xc", c, kc))
                        wait(("var2", c))
                        rhs = [xcv(kc) for kc in range(ECN)]
                        ps = psA
                    elif stage == "w_c":
                        wait(("var2", c))
                        rhs = [xcv(kc) for kc in range(ECN)]
                        ps = psB
                    elif stage == "gate":
                        rhs = [xnv(kc) for kc in range(FC)]
                        ps = psA if oc == 0 else psB
                        wait(("bpadev", c) if oc == 0 else ("cpadev", c))
                    elif stage == "w_delta":
                        rhs = [xcv(kc) for kc in range(ECN)]
                    elif stage == "w1e0":
                        wait(("mx0", c))
                        rhs = [xlnv(kc) for kc in range(FC)]
                    elif stage == "w1e1":
                        wait(("mx1", c))
                        rhs = [xlnv(kc) for kc in range(FC)]
                    elif stage == "sw1":
                        rhs = [xnv(kc) for kc in range(FC)]
                    elif stage in ("w2e0h0", "w2e0h1", "w2e1h0", "w2e1h1",
                                   "sw2h0", "sw2h1"):
                        rhs = []
                        for kc in range(16):
                            wait(("h1", c, stage, kc))
                            rhs.append(poolv(kc, 0, TC))
                    elif stage == "w_out":
                        while yg_next[0] < 4 * ECN:
                            emit_y_ng()
                        for ec in range(ECN):
                            wait(("yg", c, ec))
                        rhs = [szv(kc) for kc in range(ECN)]
                    for kc in range(K):
                        lt = take_tile(c)
                        mm(ps[:], lt, rhs[kc], kc == 0, kc == K - 1,
                           gkey if kc == K - 1 else None)
                    if tcur[0] % 16 == 0:
                        Pr.alias(("wfree_g", (tcur[0] - 1) // 16), gkey)

                    if stage == "w_c":
                        wait(("bpadev", c))
                        for n in range(NST):
                            if n >= 2:
                                wait(("psyfree", c, 4 + n - 2))
                            else:
                                wait(("psyfree", c, 2 + n))
                            mm(psY[n % 2][:], selv(n), bpad[:], True, True,
                               ("pebr", c, n))
                        wait(("cpadev", c))
                        for n in range(NST):
                            wait(("psyfree", c, 20 + n - 2))
                            mm(psY[n % 2][:], selv(n), cpad[:], True, True,
                               ("pecr", c, n))
                    if stage == "gate" and oc == 1:
                        wait(("mask0", c))
                        wait(("psyfree", c, 34))
                        mm(psY[0][:], onesrf[:], rowv(2), True, True, ("pebc", c, 4))
                        wait(("mask1", c))
                        wait(("psyfree", c, 35))
                        mm(psY[1][:], onesrf[:], rowv(0), True, True, ("pebc", c, 5))
                    if stage == "w_delta" and oc >= 1:
                        target = 4 * (oc - 1)
                        while yg_next[0] < min(target, 4 * ECN):
                            if not emit_y_ng():
                                break
                while yg_next[0] < 4 * ECN:
                    emit_y_ng()

        def emit_yevac(Pr, eng, op, wait, c, ec):
            wait(("pey", c, ec))
            wait(("bankfree", c, G_ROT[("w_in", ECN + ec)]))
            bank = ec % 2
            op("stt", scrYv(ec % 2), xcv(ec), smc("Dcol", ec), psY[bank][:],
               op0=ALU.mult, op1=ALU.add)
            op("stt", szv(ec), scrYv(ec % 2), 1.0, szv(ec), op0=ALU.mult,
               op1=ALU.mult, inckey=("yg", c, ec))
            Pr.alias(("psyfree", c, TEN_Y0 + ec), ("yg", c, ec))

        def emit_vector(Pr, eng):
            wait = lambda k: Pr.wait("vector", eng, k)
            dry = Pr.dry

            def op(kind, *a, inckey=None, **kw):
                if dry:
                    instr = None
                else:
                    f = {"stt": eng.scalar_tensor_tensor, "ts": eng.tensor_scalar,
                         "cp": eng.tensor_copy, "scan": eng.tensor_tensor_scan,
                         "ms": eng.memset}[kind]
                    instr = f(*a, **kw)
                if inckey is not None:
                    Pr.inc(instr, "ve", inckey, 1)
                return instr

            wait(("iotas",))
            wait(("sm",))
            for n in range(NST):
                op("ts", selv(n), iop[:].broadcast_to([P, P]), float(n), None,
                   op0=ALU.is_equal)
            op("ts", ident[:], scrA[:, 0:P], iop[:, 0:1], None, op0=ALU.is_equal,
               inckey=("selinit",))
            op("ms", onescf[:, 1:P], 0.0)
            op("ms", onescf[:, 0:1], 1.0)
            op("ms", onescb[:, 1:P], 0.0)
            op("ms", onescb[:, 0:1], 1.0)
            op("ms", onesrf[:], 1.0)
            for _i, _v in enumerate((EPS, 1.0, -0.5, 0.0)):
                op("ms", consts[:, _i:_i + 1], _v,
                   inckey=("cinit",) if _i == 3 else None)
            op("ms", carry[:], 0.0)
            for ec in range(ECN):
                op("ms", poolv(ec, 0, 3), 0.0, inckey=("minit", ec) if ec == ECN - 1 else None)
            Pr.alias(("ones",), ("minit", ECN - 1))
            op("stt", smc("bsum", 0, FC), smc("b2e0", 0, FC), 1.0, smc("b2e1", 0, FC),
               op0=ALU.mult, op1=ALU.add)
            op("stt", smc("bsum", 0, FC), smc("bsum", 0, FC), 1.0, smc("sb2", 0, FC),
               op0=ALU.mult, op1=ALU.add, inckey=("bsum",))
            wait(("acolsexp",))
            op("ts", acols[:], acols[:], -1.0, None, op0=ALU.mult,
               inckey=("acolsneg",))
            for i in range(NPREP):
                wait(("prepin", i))
                if i >= 1:
                    wait(("prepout", i - 1))
                op("cp", xnb[:, 0:SL], xbuf[:, 0:SL], inckey=("prepcast", i))

            for c in range(nch):
                for ec in range(ECN):
                    if c >= 1:
                        wait(("peg", c - 1, "sw2h1", 7))
                        op("cp", poolv(ec, 0, 3), poolv(ec, 512, 515),
                           inckey=("tail", c, ec))
                    else:
                        Pr.alias(("tail", c, ec), ("minit", ECN - 1))
                if c == 0:
                    wait(("prepout", NPREP - 1))
                wait(("peSx", c))
                wait(("peSx2", c))
                if c > 0:
                    wait(("pebc", c - 1, 4))
                    wait(("pebc", c - 1, 5))
                op("ts", rowv(0), psA[0:1, :], INV_D, None, op0=ALU.mult)
                op("ts", rowv(1), psB[0:1, :], INV_D, None, op0=ALU.mult)
                op("stt", rowv(2), rowv(0), 1.0, rowv(0), op0=ALU.mult, op1=ALU.mult)
                op("stt", rowv(1), rowv(1), 1.0, rowv(2), op0=ALU.mult,
                   op1=ALU.subtract, inckey=("var1", c))
                wait(("rstd1", c))
                op("stt", rowv(0), rowv(0), 1.0, rowv(1), op0=ALU.mult, op1=ALU.mult,
                   inckey=("mr1", c))
                wait(("pebc", c, 0))
                op("cp", repv(0), psY[0][:], inckey=("psyfree", c, 0))
                wait(("pebc", c, 1))
                op("cp", repv(1), psY[1][:], inckey=("psyfree", c, 1))
                if c >= 1:
                    wait(("peg", c - 1, "sw1", 31))
                for fc in range(FC):
                    op("stt", scrYv(fc % 2), xv(fc), 1.0, repv(0), op0=ALU.mult,
                       op1=ALU.mult)
                    op("stt", xnv(fc), scrYv(fc % 2), 1.0, repv(1), op0=ALU.mult,
                       op1=ALU.subtract, inckey=("xnfc", c, fc))
                Pr.alias(("xn", c), ("xnfc", c, FC - 1))
                wait(("peSxn", c))
                wait(("peSxn2", c))
                op("ts", rowv(0), psA[0:1, :], INV_D, None, op0=ALU.mult)
                op("ts", rowv(1), psB[0:1, :], INV_D, None, op0=ALU.mult)
                op("stt", rowv(2), rowv(0), 1.0, rowv(0), op0=ALU.mult, op1=ALU.mult)
                op("stt", rowv(1), rowv(1), 1.0, rowv(2), op0=ALU.mult,
                   op1=ALU.subtract, inckey=("var2", c))
                wait(("rstd2", c))
                op("stt", rowv(0), rowv(0), 1.0, rowv(1), op0=ALU.mult, op1=ALU.mult,
                   inckey=("mr2", c))
                wait(("pebc", c, 2))
                op("cp", repv(0), psY[0][:], inckey=("psyfree", c, 2))
                wait(("pebc", c, 3))
                op("cp", repv(1), psY[1][:], inckey=("psyfree", c, 3))
                if c >= 1:
                    wait(("peg", c - 1, "w1e1", 31))
                for fc in range(FC):
                    op("stt", scrYv(fc % 2), xnv(fc), 1.0, repv(0), op0=ALU.mult,
                       op1=ALU.mult)
                    op("stt", scrYv(fc % 2), scrYv(fc % 2), 1.0, repv(1), op0=ALU.mult,
                       op1=ALU.subtract)
                    op("ts", xlnv(fc), scrYv(fc % 2), smc("wnorm", fc), smc("bnorm", fc),
                       op0=ALU.mult, op1=ALU.add,
                       inckey=("xln", c) if fc == FC - 1 else None)
                if c >= 1:
                    wait(("gsilu", c - 1))
                    wait(("xre", c - 1))
                    for fc in range(FC):
                        op("stt", scrYv(fc % 2), sv(fc), 1.0, gv(fc), op0=ALU.mult,
                           op1=ALU.mult)
                        op("stt", xv(fc), scrYv(fc % 2), 1.0, xv(fc), op0=ALU.mult,
                           op1=ALU.add,
                           inckey=("combine", c - 1) if fc == FC - 1 else None)
                for ec in range(ECN):
                    wait(("xpev", c, ec))
                    wait(("peSx2", c))
                    if ec >= 2:
                        wait(("xc", c, ec - 2))
                    op("ts", scrAv(ec % 2), poolv(ec, 0, TC), smc("w4", ec * 4 + 0),
                       smc("bconv", ec), op0=ALU.mult, op1=ALU.add)
                    for k in range(1, 4):
                        op("stt", scrAv(ec % 2), poolv(ec, k, TC + k),
                           smc("w4", ec * 4 + k), scrAv(ec % 2),
                           op0=ALU.mult, op1=ALU.add,
                           inckey=("conv", c, ec) if k == 3 else None)
                wait(("peg", c, "gate", 0))
                wait(("peg", c, "gate", 1))
                op("cp", rowv(0), psA[0:1, :], inckey=("gatefree0", c))
                op("cp", rowv(1), psB[0:1, :], inckey=("gatefree1", c))
                op("stt", rowv(2), rowv(0), 1.0, rowv(1), op0=ALU.mult,
                   op1=ALU.subtract)
                op("ts", rowv(2), rowv(2), 0.0, None, op0=ALU.is_ge,
                   inckey=("mask0", c))
                op("ts", rowv(0), rowv(2), -1.0, 1.0, op0=ALU.mult, op1=ALU.add,
                   inckey=("mask1", c))
                wait(("pebc", c, 4))
                op("cp", maskv(0), psY[0][:], inckey=("psyfree", c, 36))
                wait(("pebc", c, 5))
                op("cp", maskv(1), psY[1][:], inckey=("psyfree", c, 37))
                tasks = []
                for stage, oc in PE_GROUPS:
                    if stage in ("w2e0h1", "w2e1h0", "w2e1h1", "sw2h0", "sw2h1"):
                        tasks.append((GPOS[(stage, oc)], stage, oc))
                ti = [0]

                def drain_tasks(upto_pos):
                    while ti[0] < len(tasks) and tasks[ti[0]][0] <= upto_pos:
                        _, stage, oc = tasks[ti[0]]
                        r = G_ROT[(stage, oc)]
                        wait(("peg", c, stage, oc))
                        wait(("bankfree", c, G_ROT[("w2e0h0", oc)]))
                        op("stt", gv(oc), gv(oc), 1.0, psM[R2[(stage, oc)] % 4][:],
                           op0=ALU.mult, op1=ALU.add, inckey=("bankfree", c, r))
                        ti[0] += 1

                def emit_carry(sec, ng):
                    op("cp", carry[:, sec * NST + ng * 4: sec * NST + ng * 4 + 4],
                       h4v(ng).rearrange("p (n t) -> p n t", n=4)[:, :, TC - 1],
                       inckey=("h4", c, sec, ng))

                def b4_emit(bec, ng):
                    u_b = uv(bec % 2).unsqueeze(1).broadcast_to([P, 4, TC])
                    for n in range(ng * 4, ng * 4 + 4):
                        wait(("brevac", c, n))
                    op("stt", b4v(ng % 2).rearrange("p (g t) -> p g t", g=4),
                       u_b, 1.0,
                       brep[:, ng * 4 * TC:(ng + 1) * 4 * TC].rearrange(
                           "p (g t) -> p g t", g=4),
                       op0=ALU.mult, op1=ALU.mult, inckey=("b4", c, bec, ng))

                def emit_scans(sec):
                    have_next = hasattr(emit_scans, "post_g1")
                    for ng in range(4):
                        g = sec * 4 + ng
                        gg = c * 4 * ECN + g
                        if gg >= 4:
                            wait(peygkey(gg - 4))
                        for j in range(4):
                            n = ng * 4 + j
                            gn = sec * NST + n
                            wait(("a", c, sec, n))
                            op("scan", h4v(ng)[:, j * TC:(j + 1) * TC],
                               arv(gn % 8), b4v(ng % 2)[:, j * TC:(j + 1) * TC],
                               carry[:, sec * NST + n: sec * NST + n + 1],
                               op0=ALU.mult, op1=ALU.add)
                        if ng == 0:
                            b4_emit(sec, 2)
                        elif ng == 1:
                            if have_next:
                                emit_scans.post_g1()
                            b4_emit(sec, 3)
                        elif have_next:
                            b4_emit(sec + 1, ng - 2)
                        else:
                            op("cp", spv(0), xnv(0))
                        emit_carry(sec, ng)

                for ec in range(ECN):
                    drain_tasks(WD_POS[ec])
                    r = G_ROT[("w_delta", ec)]
                    wait(("peg", c, "w_delta", ec))
                    gec = c * ECN + ec
                    if gec >= 1:
                        c2, e2 = divmod(gec - 1, ECN)
                        wait(("spln", c2, e2))
                    if gec >= 3:
                        c2, e2 = divmod(gec - 3, ECN)
                        wait(("a", c2, e2, NST - 1))
                    op("ts", spv(0), (psA if ec % 2 == 0 else psB)[:],
                       smc("bdelta", ec), None,
                       op0=ALU.add, inckey=("bankfree", c, r))
                    op("ts", spv(1), spv(0), 0.0, None, op0=ALU.max)
                    op("stt", spv(2), spv(0), -1.0, spv(0), op0=ALU.mult,
                       op1=ALU.min, inckey=("spna", c, ec))
                    def sp2u(ec=ec, gec=gec):
                        wait(("spln", c, ec))
                        op("stt", dlv(ec), spv(1), 1.0, spv(2), op0=ALU.mult,
                           op1=ALU.add, inckey=("delta", c, ec))
                        wait(("xc", c, ec))
                        op("stt", uv(ec % 2), dlv(ec), 1.0, xcv(ec), op0=ALU.mult,
                           op1=ALU.mult, inckey=("u", c, ec))
                    if ec >= 1:
                        emit_scans.post_g1 = sp2u
                        emit_scans(ec - 1)
                        del emit_scans.post_g1
                    else:
                        sp2u()
                        b4_emit(0, 0)
                        b4_emit(0, 1)
                    if ec >= 2:
                        emit_yevac(Pr, eng, op, wait, c, ec - 2)
                emit_scans(ECN - 1)
                for ec in (ECN - 2, ECN - 1):
                    emit_yevac(Pr, eng, op, wait, c, ec)
                drain_tasks(10 ** 9)
                for oc in range(8):
                    r = G_ROT[("w_out", oc)]
                    wait(("peg", c, "w_out", oc))
                    op("stt", sv(oc), xnv(oc), 1.0,
                       psM[R2[("w_out", oc)] % 4][:], op0=ALU.mult,
                       op1=ALU.add, inckey=("bankfree", c, r))
                    if oc == 7:
                        Pr.alias(("schunk", c), ("bankfree", c, r))
            c = nch
            wait(("gsilu", c - 1))
            wait(("xre", c - 1))
            for fc in range(FC):
                op("stt", scrYv(fc % 2), sv(fc), 1.0, gv(fc), op0=ALU.mult,
                   op1=ALU.mult)
                op("stt", xv(fc), scrYv(fc % 2), 1.0, xv(fc), op0=ALU.mult,
                   op1=ALU.add,
                   inckey=("combine", c - 1) if fc == FC - 1 else None)

        def emit_scalar(Pr, eng):
            wait = lambda k: Pr.wait("scalar", eng, k)
            dry = Pr.dry

            def op(kind, *a, inckey=None, **kw):
                if dry:
                    instr = None
                else:
                    f = {"act": eng.activation, "cp": eng.copy}[kind]
                    instr = f(*a, **kw)
                if inckey is not None:
                    Pr.inc(instr, "se", inckey, 1)
                return instr

            wait(("sm",))
            wait(("cinit",))
            op("act", acols[:], smc("Alog", 0, ECN * NST), AF.Exp,
               inckey=("acolsexp",))

            for c in range(nch):
                wait(("xin", c))
                if c == 0:
                    wait(("selinit",))
                if c > 0:
                    wait(("pebc", c - 1, 4))
                    wait(("pebc", c - 1, 5))
                if c > 0:
                    wait(("peg", c - 1, "w_out", 7))
                    wait(("schunk", c - 1))
                for fc in range(FC):
                    if fc >= 4:
                        wait(("peSx2f", c, fc - 4))
                    op("act", xsqv(fc), xv(fc), AF.Square,
                       inckey=("xsq", c, fc))
                wait(("var1", c))
                op("act", rowv(2), rowv(1), AF.Ln, bias=EPS)
                op("act", rowv(1), rowv(2), AF.Exp, scale=-0.5,
                   inckey=("rstd1", c))
                if c > 0:
                    wait(("peg", c - 1, "w1e1", 31))
                for fc in range(FC):
                    wait(("xnfc", c, fc))
                    if fc >= 4:
                        wait(("peSxn2f", c, fc - 4))
                    op("act", scrQv(fc), xnv(fc), AF.Square,
                       inckey=("xnsq", c, fc))
                wait(("var2", c))
                wait(("pebc", c, 0))
                op("act", rowv(2), rowv(1), AF.Ln, bias=EPS)
                op("act", rowv(1), rowv(2), AF.Exp, scale=-0.5,
                   inckey=("rstd2", c))
                if c >= 1:
                    wait(("w2done", c - 1))
                    wait(("combine2", c - 1)) if False else None
                    for fc in range(FC):
                        op("act", gv(fc), gv(fc), AF.Silu, bias=smc("bsum", fc),
                           inckey=("gsilu", c - 1) if fc == FC - 1 else None)
                for oc in range(32):
                    r = G_ROT[("w_in", oc)]
                    wait(("peg", c, "w_in", oc))
                    if oc < ECN:
                        wait(("tail", c, oc))
                        op("cp", poolv(oc, 3, 515), psM[R2[("w_in", oc)] % 4][:],
                           inckey=("bankfree", c, r))
                        Pr.alias(("xpev", c, oc), ("bankfree", c, r))
                    else:
                        if c >= 1:
                            wait(("peg", c - 1, "w_out", 7))
                        op("act", szv(oc - ECN), psM[R2[("w_in", oc)] % 4][:],
                           AF.Silu, inckey=("bankfree", c, r))
                for ec in range(ECN):
                    wait(("conv", c, ec))
                    if c >= 1:
                        wait(("peg", c - 1, "w_delta", 15))
                        wait(("yg", c - 1, ec))
                    op("act", xcv(ec), scrAv(ec % 2), AF.Silu,
                       inckey=("xc", c, ec))
                wait(("peg", c, "w_b", 0))
                if c >= 1:
                    wait(("pebr", c - 1, NST - 1))
                op("cp", bpad[:], psA[:], inckey=("bpadev", c))
                wait(("peg", c, "w_c", 0))
                if c >= 1:
                    wait(("pecr", c - 1, NST - 1))
                op("cp", cpad[:], psB[:], inckey=("cpadev", c))
                for n in range(NST):
                    wait(("pebr", c, n))
                    if c >= 1:
                        wait(("b4", c - 1, ECN - 1, n // 4))
                    op("cp", brv(n), psY[n % 2][:], inckey=("brevac", c, n))
                    Pr.alias(("psyfree", c, 4 + n), ("brevac", c, n))
                for n in range(NST):
                    wait(("pecr", c, n))
                    if c >= 1:
                        wait(("q4", c - 1, ECN - 1, n // 4))
                    op("cp", crv(n), psY[n % 2][:], inckey=("crevac", c, n))
                    Pr.alias(("psyfree", c, 20 + n), ("crevac", c, n))
                pend_a = []

                def emit_a_group(ec, ng):
                    wait(("delta", c, ec))
                    wait(("acolsneg",))
                    for n in range(ng * 4, ng * 4 + 4):
                        gan = (c * ECN + ec) * NST + n
                        if gan >= 8:
                            prevg = (gan - 8) // 4
                            wait(sgkey(prevg))
                        op("act", arv((ec * NST + n) % 8), dlv(ec), AF.Exp,
                           scale=acols[:, ec * NST + n: ec * NST + n + 1],
                           inckey=("a", c, ec, n))

                def drain_a(k=1):
                    while k > 0 and pend_a:
                        emit_a_group(*pend_a.pop(0))
                        k -= 1

                def emit_a(ec):
                    for ng in range(4):
                        emit_a_group(ec, ng)

                for stage, oc in PE_GROUPS:
                    if stage == "w_delta":
                        ec = oc
                        wait(("spna", c, ec))
                        op("act", spv(0), spv(2), AF.Exp)
                        op("act", spv(2), spv(0), AF.Ln, bias=1.0,
                           inckey=("spln", c, ec))
                        if ec >= 1:
                            emit_a(ec - 1)
                    elif stage in ("w1e0", "w1e1", "sw1"):
                        drain_a(1)
                        kc = oc % 16
                        half = oc // 16
                        r = G_ROT[(stage, oc)]
                        wait(("peg", c, stage, oc))
                        if stage == "w1e0" and half == 0:
                            wait(("conv", c, kc))
                        else:
                            wait(("peg", c, H1_PREV[(stage, half)], 7))
                        bname = {"w1e0": "b1e0", "w1e1": "b1e1", "sw1": "sb1"}[stage]
                        path = {"w1e0": "e0", "w1e1": "e1", "sw1": "sh"}[stage]
                        if DBG_TAPS and c == TAPC and (stage, half) in H1_PREV:
                            wait(("tapdma", (stage, half), kc))
                        op("act", poolv(kc, 0, TC), psM[R2[(stage, oc)] % 4][:],
                           AF.Relu, bias=smc(bname, half * 16 + kc),
                           inckey=("bankfree", c, r))
                        Pr.alias(("h1", c, W2_OF[path] + f"h{half}", kc),
                                 ("bankfree", c, r))
                    elif stage == "w2e0h0":
                        drain_a(1)
                        r = G_ROT[(stage, oc)]
                        wait(("peg", c, stage, oc))
                        if c >= 1 and oc == 0:
                            wait(("combine", c - 1))
                        op("cp", gv(oc), psM[R2[(stage, oc)] % 4][:],
                           inckey=("bankfree", c, r))
                drain_a(99)
                emit_a(ECN - 1)
                Pr.alias(("w2done", c), ("bankfree", c, G_ROT[("sw2h1", 7)]))
            c = nch
            wait(("w2done", c - 1))
            for fc in range(FC):
                op("act", gv(fc), gv(fc), AF.Silu, bias=smc("bsum", fc),
                   inckey=("gsilu", c - 1) if fc == FC - 1 else None)

        def emit_all(Pr, engs):
            emit_sync(Pr, engs.get("sync"))
            emit_gpsimd(Pr, engs.get("gpsimd"))
            emit_tensor(Pr, engs.get("tensor"))
            emit_vector(Pr, engs.get("vector"))
            emit_scalar(Pr, engs.get("scalar"))

        dryP = Prog(dry=True)
        emit_all(dryP, {})
        realP = Prog(dry=False, events=dryP.events, sems=sems)

        @block.sync
        def _(sync):
            emit_sync(realP, sync)

        @block.gpsimd
        def _(g):
            emit_gpsimd(realP, g)

        @block.tensor
        def _(t):
            emit_tensor(realP, t)

        @block.vector
        def _(v):
            emit_vector(realP, v)

        @block.scalar
        def _(s):
            emit_scalar(realP, s)

        for k in ("sy", "pe", "ve", "se", "po", "pd"):
            assert dryP.cnt[k] == realP.cnt[k], (k, dryP.cnt[k], realP.cnt[k])
    return nc


def kernel(**inputs):
    x = np.asarray(inputs["L0_attn_x"])
    wt = host_pack_wtiles(inputs)
    sm = host_pack_smalls(inputs)
    nch = x.shape[1] // TC
    nc = build_nc(nch)
    in_maps = []
    for b in range(x.shape[0]):
        in_maps.append({
            "xT": np.ascontiguousarray(x[b].T),
            "wt": wt,
            "smI": sm,
        })
    res = run_bass_kernel_spmd(nc, in_maps, core_ids=list(range(x.shape[0])))
    out = np.stack([np.asarray(res.results[b]["outT"]).T for b in range(x.shape[0])], axis=0)
    return out


# version 08
# speedup vs baseline: 1.2154x; 1.2119x over previous
import contextlib
import math
import numpy as np

import concourse.bass as bass
import concourse.mybir as mybir
from concourse.bass_utils import run_bass_kernel_spmd

F32 = mybir.dt.float32
BF16 = mybir.dt.bfloat16
AF = mybir.ActivationFunctionType
ALU = mybir.AluOpType

D, E, HID, NST = 1024, 2048, 4096, 16
DBG_TAPS = False
DBG2 = False
TAPC = 1
FC, ECN, HCN = 8, 16, 32
TC = 512
P = 128
NCH_FULL = 16
EPS = 1e-5
INV_D = 1.0 / D
LOG_EPS = float(math.log(EPS))
CONSTS = (EPS, 1.0, -0.5, 0.0, -1.0, LOG_EPS, 1.0 + EPS)

GROUP_K = {
    "w_in": 8, "w_b": 16, "w_c": 16, "gate": 8, "w_delta": 16,
    "w1e0": 8, "w1e1": 8, "sw1": 8,
    "w2e0h0": 16, "w2e0h1": 16, "w2e1h0": 16, "w2e1h1": 16,
    "sw2h0": 16, "sw2h1": 16, "w_out": 16,
}
MAIN_STAGES = {"w_in", "w_delta", "w1e0", "w1e1", "sw1", "w2e0h0", "w2e0h1",
               "w2e1h0", "w2e1h1", "sw2h0", "sw2h1", "w_out"}
W1_OF = {"e0": "w1e0", "e1": "w1e1", "sh": "sw1"}
W2_OF = {"e0": "w2e0", "e1": "w2e1", "sh": "sw2"}
H1_PREV = {("w1e0", 1): "w2e0h0", ("w1e1", 0): "w2e0h1", ("w1e1", 1): "w2e1h0",
           ("sw1", 0): "w2e1h1", ("sw1", 1): "sw2h0"}


def pe_group_sequence():
    seq = [("w_in", oc) for oc in range(32)]
    seq += [("w_b", 0), ("w_c", 0), ("gate", 0), ("gate", 1)]
    ffn = []
    for path in ("e0", "e1", "sh"):
        for half in range(2):
            ffn += [(W1_OF[path], oc) for oc in range(16 * half, 16 * half + 16)]
            ffn += [(W2_OF[path] + f"h{half}", oc) for oc in range(8)]
    inter = [("w_delta", 0), ("w_delta", 1), ("w_delta", 2)]
    nxt = 3
    total_t = sum(GROUP_K[g[0]] for g in ffn)
    cum = 0
    for i, g in enumerate(ffn):
        inter.append(g)
        cum += GROUP_K[g[0]]
        if nxt < ECN and cum % 16 == 0 and cum >= (nxt - 2) * total_t / 13.0:
            inter.append(("w_delta", nxt))
            nxt += 1
    assert nxt == ECN, nxt
    seq += inter
    seq += [("w_out", oc) for oc in range(8)]
    return seq


PE_GROUPS = pe_group_sequence()
MAIN_GROUPS = [g for g in PE_GROUPS if g[0] in MAIN_STAGES]
NMAIN = len(MAIN_GROUPS)
G_ROT = {g: i for i, g in enumerate(MAIN_GROUPS)}
NONWD = [g for g in MAIN_GROUPS if g[0] != "w_delta"]
R2 = {g: i for i, g in enumerate(NONWD)}
NR2 = len(NONWD)
GPOS = {g: i for i, g in enumerate(PE_GROUPS)}
WD_POS = {oc: GPOS[("w_delta", oc)] for oc in range(ECN)}
FFN_STAGES = {"w1e0", "w1e1", "sw1", "w2e0h0", "w2e0h1", "w2e1h0", "w2e1h1",
              "sw2h0", "sw2h1"}


def wtile_order():
    L = []
    for stage, oc in PE_GROUPS:
        for kc in range(GROUP_K[stage]):
            L.append((stage, oc, kc))
    assert len(L) % 16 == 0
    return L


WTILES = wtile_order()
NT = len(WTILES)
NSLOT = NT // 16
RING = 2
TEN_Y0 = 38
SL = 2048
NPREP = NT * P // SL
assert NPREP * SL == NT * P


def host_pack_wtiles(inp):
    ws = {
        "w_in": inp["L0_attn_w_in"], "w_b": inp["L0_attn_w_b"],
        "w_c": inp["L0_attn_w_c"], "w_delta": inp["L0_attn_w_delta"],
        "w_out": inp["L0_attn_w_out"], "gate_w": inp["L0_ffn_gate_w"],
        "w1": inp["L0_ffn_w1"], "w2": inp["L0_ffn_w2"],
        "sw1": inp["L0_ffn_sw1"], "sw2": inp["L0_ffn_sw2"],
    }
    ws = {k: np.asarray(v) for k, v in ws.items()}
    arr = np.zeros((NT, P, P), dtype=np.float32)
    for i, (stage, oc, kc) in enumerate(WTILES):
        if stage == "w_in":
            arr[i] = ws["w_in"][kc * P:(kc + 1) * P, oc * P:(oc + 1) * P]
        elif stage == "w_b":
            arr[i, :, 0:NST] = ws["w_b"][kc * P:(kc + 1) * P, :]
        elif stage == "w_c":
            arr[i, :, 0:NST] = ws["w_c"][kc * P:(kc + 1) * P, :]
        elif stage == "gate":
            arr[i, :, 0] = ws["gate_w"][oc, kc * P:(kc + 1) * P]
        elif stage == "w_delta":
            arr[i] = ws["w_delta"][kc * P:(kc + 1) * P, oc * P:(oc + 1) * P]
        elif stage in ("w1e0", "w1e1", "sw1"):
            w = {"w1e0": ws["w1"][0], "w1e1": ws["w1"][1], "sw1": ws["sw1"]}[stage]
            arr[i] = w[kc * P:(kc + 1) * P, oc * P:(oc + 1) * P]
        elif stage[:-2] in ("w2e0", "w2e1", "sw2"):
            w = {"w2e0": ws["w2"][0], "w2e1": ws["w2"][1], "sw2": ws["sw2"]}[stage[:-2]]
            kcg = int(stage[-1]) * 16 + kc
            arr[i] = w[kcg * P:(kcg + 1) * P, oc * P:(oc + 1) * P]
        elif stage == "w_out":
            arr[i] = ws["w_out"][kc * P:(kc + 1) * P, oc * P:(oc + 1) * P]
    return np.ascontiguousarray(arr.transpose(1, 0, 2).reshape(P, NT * P))


SM = {}
_off = 0
for _n, _w in [("wnorm", FC), ("bnorm", FC), ("w4", ECN * 4), ("bconv", ECN),
               ("bdelta", ECN), ("Dcol", ECN), ("Alog", ECN * NST),
               ("b1e0", HCN), ("b1e1", HCN), ("sb1", HCN),
               ("b2e0", FC), ("b2e1", FC), ("sb2", FC), ("bsum", FC)]:
    SM[_n] = (_off, _off + _w)
    _off += _w
SMW = _off


def host_pack_smalls(inp):
    sm = np.zeros((P, SMW), dtype=np.float32)

    def put(name, vec, w):
        a, b = SM[name]
        sm[:, a:b] = np.asarray(vec).reshape(w, P).T

    put("wnorm", inp["L0_attn_w_norm"], FC)
    put("bnorm", inp["L0_attn_b_norm"], FC)
    a, b = SM["w4"]
    sm[:, a:b] = np.asarray(inp["L0_attn_w_conv"]).reshape(ECN, P, 4).transpose(1, 0, 2).reshape(P, ECN * 4)
    put("bconv", inp["L0_attn_b_conv"], ECN)
    put("bdelta", inp["L0_attn_b_delta"], ECN)
    put("Dcol", inp["L0_attn_D_param"], ECN)
    a, b = SM["Alog"]
    sm[:, a:b] = np.asarray(inp["L0_attn_A_log"]).reshape(ECN, P, NST).transpose(1, 0, 2).reshape(P, ECN * NST)
    put("b1e0", inp["L0_ffn_b1"][0], HCN)
    put("b1e1", inp["L0_ffn_b1"][1], HCN)
    put("sb1", inp["L0_ffn_sb1"], HCN)
    put("b2e0", inp["L0_ffn_b2"][0], FC)
    put("b2e1", inp["L0_ffn_b2"][1], FC)
    put("sb2", inp["L0_ffn_sb2"], FC)
    return sm


class Prog:
    def __init__(self, dry, events=None, sems=None):
        self.dry = dry
        self.cnt = {k: 0 for k in ("sy", "pe", "ve", "se", "po", "pd")}
        self.events = events if events is not None else {}
        self.sems = sems or {}
        self.waited = {}

    def inc(self, instr, sem, key, n=1):
        self.cnt[sem] += n
        ev = (sem, self.cnt[sem])
        if self.dry:
            assert key not in self.events, f"dup event {key}"
            self.events[key] = ev
        else:
            assert self.events.get(key) == ev, (key, self.events.get(key), ev)
            if instr is not None:
                instr.then_inc(self.sems[sem], n)

    def alias(self, newkey, oldkey):
        if self.dry:
            self.events[newkey] = self.events[oldkey]

    def wait(self, engname, eng, key):
        if self.dry:
            return
        sem, val = self.events[key]
        w = self.waited.setdefault(engname, {})
        if w.get(sem, -1) >= val:
            return
        w[sem] = val
        eng.wait_ge(self.sems[sem], val)


def build_nc(nch=NCH_FULL):
    T = nch * TC
    nc = bass.Bass("TRN2", target_bir_lowering=False)

    xT = nc.dram_tensor("xT", [D, T], F32, kind="ExternalInput").ap()
    wt = nc.dram_tensor("wt", [P, NT * P], F32, kind="ExternalInput").ap()
    smI = nc.dram_tensor("smI", [P, SMW], F32, kind="ExternalInput").ap()
    outT = nc.dram_tensor("outT", [D, T], F32, kind="ExternalOutput").ap()
    dbg = nc.dram_tensor("dbg", [P, 8 * 8192], BF16, kind="ExternalOutput").ap() if DBG_TAPS else None
    dbg2 = nc.dram_tensor("dbg2", [P, 4096], F32, kind="ExternalOutput").ap() if DBG2 else None
    wsb = nc.dram_tensor("wsb", [P, NT * P], BF16, kind="Internal").ap()

    ctx = contextlib.ExitStack()
    with ctx:
        sb = lambda name, shape, dt: ctx.enter_context(nc.sbuf_tensor(name, shape, dt))
        wring = sb("wring", [P, RING * 16 * P], BF16)
        xbuf = sb("xbuf", [P, FC * TC], F32)
        xnb = sb("xnb", [P, FC * TC], BF16)
        xlnmx = sb("xlnmx", [P, FC * TC], BF16)
        pool = sb("pool", [P, ECN * 516], BF16)
        szb = sb("szb", [P, ECN * TC], BF16)
        xcyg = sb("xcyg", [P, ECN * TC], BF16)
        deltar = sb("deltar", [P, 3 * TC], BF16)
        brep = sb("brep", [P, NST * TC], BF16)
        crep = sb("crep", [P, NST * TC], BF16)
        bpad = sb("bpad", [P, TC], BF16)
        cpad = sb("cpad", [P, TC], BF16)
        aring = sb("aring", [P, 8 * TC], BF16)
        b4r = sb("b4r", [P, 2 * 4 * TC], BF16)
        h4r = sb("h4r", [P, 4 * 4 * TC], BF16)
        ur = sb("ur", [P, 2 * TC], BF16)
        gacc = sb("gacc", [P, FC * TC], BF16)
        sres = sb("sres", [P, FC * TC], BF16)
        carry = sb("carry", [P, ECN * NST], F32)
        smalls = sb("smalls", [P, SMW], F32)
        acols = sb("acols", [P, ECN * NST], F32)
        selt = sb("selt", [P, NST * P], BF16)
        ident = sb("ident", [P, P], BF16)
        onescf = sb("onescf", [P, P], F32)
        onescb = sb("onescb", [P, P], BF16)
        onesrf = sb("onesrf", [1, P], F32)
        maskr = sb("maskr", [P, 2 * TC], BF16)
        repf = sb("repf", [P, 2 * TC], F32)
        scrA = sb("scrA", [P, 2 * TC], F32)
        scrSP = sb("scrSP", [P, 3 * TC], BF16)
        scrY = sb("scrY", [P, TC], F32)
        rows = sb("rows", [1, 3 * TC], F32)
        iop = sb("iop", [P, 1], F32)
        consts = sb("consts", [P, 8], F32)

        psM = [ctx.enter_context(nc.psum_tensor(f"psM{i}", [P, TC], F32)) for i in range(4)]
        psY = [ctx.enter_context(nc.psum_tensor(f"psY{i}", [P, TC], F32)) for i in range(2)]
        psA = ctx.enter_context(nc.psum_tensor("psA", [P, TC], F32))
        psB = ctx.enter_context(nc.psum_tensor("psB", [P, TC], F32))

        sems = {k: ctx.enter_context(nc.semaphore(k))
                for k in ("sy", "pe", "ve", "se", "po", "pd")}
        for _i, _v in enumerate(CONSTS):
            nc.const_aps.register(F32, _v, consts[:, _i:_i + 1])
        block = ctx.enter_context(nc.Block())

        xv = lambda fc: xbuf[:, fc * TC:(fc + 1) * TC]
        xnv = lambda fc: xnb[:, fc * TC:(fc + 1) * TC]
        xlnv = lambda fc: xlnmx[:, fc * TC:(fc + 1) * TC]
        szv = lambda ec: szb[:, ec * TC:(ec + 1) * TC]
        xcv = lambda ec: xcyg[:, ec * TC:(ec + 1) * TC]
        gv = lambda fc: gacc[:, fc * TC:(fc + 1) * TC]
        sv = lambda fc: sres[:, fc * TC:(fc + 1) * TC]
        poolv = lambda i, c0, c1: pool[:, i * 516 + c0: i * 516 + c1]
        brv = lambda n: brep[:, n * TC:(n + 1) * TC]
        crv = lambda n: crep[:, n * TC:(n + 1) * TC]
        arv = lambda s: aring[:, s * TC:(s + 1) * TC]
        b4v = lambda s: b4r[:, s * 4 * TC:(s + 1) * 4 * TC]
        h4v = lambda s: h4r[:, s * 4 * TC:(s + 1) * 4 * TC]
        uv = lambda s: ur[:, s * TC:(s + 1) * TC]
        scrSPb = scrA[:].bitcast(BF16)
        spv = lambda j: scrSPb[:, j * TC:(j + 1) * TC]
        dlv = lambda ec: deltar[:, (ec % 3) * TC:(ec % 3 + 1) * TC]
        scrAv = lambda s: scrA[:, s * TC:(s + 1) * TC]
        scrYv = lambda s: scrY[:, 0:TC]
        xsqv = lambda s: xnb[:].bitcast(F32)[:, (s % 4) * TC:(s % 4 + 1) * TC]
        scrQv = lambda s: xlnmx[:, (s % 4) * TC:(s % 4 + 1) * TC]
        rowv = lambda i: rows[0:1, i * TC:(i + 1) * TC]
        maskv = lambda i: maskr[:, i * TC:(i + 1) * TC]
        repv = lambda i: repf[:, i * TC:(i + 1) * TC]
        selv = lambda n: selt[:, n * P:(n + 1) * P]

        def smc(name, j=0, w=1):
            a, b = SM[name]
            assert a + j + w <= b
            return smalls[:, a + j:a + j + w]

        def ring_tile(t):
            tt = t % NT
            s = (t // 16) % RING
            return wring[:, (s * 16 + tt % 16) * P:(s * 16 + tt % 16 + 1) * P]

        TAP_NXT = {("w1e0", 0): ("w1e0", 1), ("w1e0", 1): ("w1e1", 0),
                   ("w1e1", 0): ("w1e1", 1), ("w1e1", 1): ("sw1", 0),
                   ("sw1", 0): ("sw1", 1), ("sw1", 1): None}
        TAP_SLOT = {}
        if DBG_TAPS:
            last_tile = {}
            for i, (st_, oc_, kc_) in enumerate(WTILES):
                last_tile[(st_, oc_)] = i
            for pi, (st_, half) in enumerate([("w1e0", 0), ("w1e0", 1), ("w1e1", 0),
                                              ("w1e1", 1), ("sw1", 0), ("sw1", 1)]):
                TAP_SLOT[last_tile[(st_, half * 16 + 15)] // 16] = (pi, st_, half)
        xTr = xT.rearrange("(f p) t -> p f t", p=P)
        outTr = outT.rearrange("(f p) t -> p f t", p=P)

        def sgkey(gg):
            c2, r = divmod(gg, 4 * ECN)
            return ("h4", c2, r // 4, r % 4)

        def q4key(gg):
            c2, r = divmod(gg, 4 * ECN)
            return ("q4", c2, r // 4, r % 4)

        def peygkey(gg):
            c2, r = divmod(gg, 4 * ECN)
            return ("peyg", c2, r // 4, r % 4)

        def akey(gan):
            c2, r = divmod(gan, NST * ECN)
            return ("a", c2, r // NST, r % NST)

        def emit_sync(Pr, eng):
            wait = lambda k: Pr.wait("sync", eng, k)

            def dma(dst, src, key):
                instr = None if Pr.dry else eng.dma_start(out=dst, in_=src)
                Pr.inc(instr, "sy", key, 16)

            dma(smalls[:], smI, ("sm",))
            for i in range(NPREP):
                if i > 0:
                    wait(("prepcast", i - 1))
                dma(xbuf[:, 0:SL], wt[:, i * SL:(i + 1) * SL], ("prepin", i))
            for c in range(nch):
                if c == 0:
                    wait(("prepout", NPREP - 1))
                dma(xbuf[:], xTr[:, :, c * TC:(c + 1) * TC], ("xin", c))
                for s in range(NSLOT):
                    g = c * NSLOT + s
                    if g >= RING:
                        wait(("wfree_g", g - RING))
                    dma(wring[:, (g % RING) * 16 * P:(g % RING + 1) * 16 * P],
                        wsb[:, s * 16 * P:(s + 1) * 16 * P], ("wslot", c, s))
                    if s == 12 and c >= 1:
                        wait(("xn", c))
                        wait(("peSx", c))
                        wait(("xsq", c, FC - 1))
                        dma(xbuf[:], xTr[:, :, (c - 1) * TC:c * TC], ("xre", c - 1))
                        wait(("combine", c - 1))
                        dma(outTr[:, :, (c - 1) * TC:c * TC], xbuf[:], ("outdma", c - 1))
                    if DBG_TAPS and c == TAPC and s in TAP_SLOT:
                        pi, st_, half = TAP_SLOT[s]
                        if (st_, half) == ("mx",):
                            pass
                        for kc in range(16):
                            r = G_ROT[(st_, half * 16 + kc)]
                            wait(("bankfree", TAPC, r))
                            nxt = TAP_NXT[(st_, half)]
                            key = ("tapdma", nxt, kc) if nxt else ("tapend", kc)
                            dma(dbg[:, (pi * 16 + kc) * TC:(pi * 16 + kc + 1) * TC],
                                poolv(kc, 0, TC), key)
                        if pi == 0:
                            wait(("mx0", TAPC))
                            dma(dbg[:, 96 * TC:104 * TC], xlnmx[:], ("tapmx0",))
                        if pi == 2:
                            wait(("mx1", TAPC))
                            dma(dbg[:, 104 * TC:112 * TC], xlnmx[:], ("tapmx1",))

            wait(("xn", nch - 1))
            wait(("peSx", nch - 1))
            wait(("xsq", nch - 1, FC - 1))
            dma(xbuf[:], xTr[:, :, (nch - 1) * TC:nch * TC], ("xre", nch - 1))
            wait(("combine", nch - 1))
            dma(outTr[:, :, (nch - 1) * TC:nch * TC], xbuf[:], ("outdma", nch - 1))
            if not Pr.dry:
                eng.wait_ge(Pr.sems["sy"], Pr.cnt["sy"])

        def emit_gpsimd(Pr, eng):
            wait = lambda k: Pr.wait("gpsimd", eng, k)
            dry = Pr.dry
            if not dry:
                eng.iota(iop[:], [[0, 1]], channel_multiplier=1,
                         allow_small_or_imprecise_dtypes=True)
                i2 = eng.iota(scrA[:, 0:P], [[1, P]], channel_multiplier=0,
                              allow_small_or_imprecise_dtypes=True)
            else:
                i2 = None
            Pr.inc(i2, "po", ("iotas",), 1)
            for i in range(NPREP):
                wait(("prepcast", i))
                instr = None if dry else eng.dma_start(
                    out=wsb[:, i * SL:(i + 1) * SL], in_=xnb[:, 0:SL])
                Pr.inc(instr, "pd", ("prepout", i), 16)
            def tap2(i, w, src_ap, key):
                if not DBG2:
                    return
                wait(w)
                instr = None if dry else eng.dma_start(
                    out=dbg2[:, i * 512:(i + 1) * 512], in_=src_ap)
                Pr.inc(instr, "pd", ("t2", key), 16)

            for c in range(nch):
                if DBG2 and c == 1:
                    tap2(0, ("tail", 1, 0), pool[:, 0:512], "tail")
                    if DBG2:
                        wait(("h4", 0, ECN - 1, 3))
                        instr = None if dry else eng.dma_start(
                            out=dbg2[:, 4 * 512:4 * 512 + 256], in_=carry[:, 0:256])
                        Pr.inc(instr, "pd", ("t2", "carry"), 16)
                    tap2(1, ("bankfree", 1, G_ROT[("w_in", 0)]), pool[:, 0:512], "xpev")
                    tap2(2, ("conv", 1, 0), scrA[:, 0:512], "cv")
                    tap2(3, ("xc", 1, 0), xcyg[:, 0:512], "xc")
                    tap2(4, ("h4", 0, ECN - 1, 3), carry[:, 0:256], "carry") if False else None
                    tap2(5, ("delta", 1, 0), deltar[:, 0:512], "delta")
                    tap2(6, ("a", 1, 0, 3), aring[:, 0:512], "a0")

        def emit_tensor(Pr, eng):
            wait = lambda k: Pr.wait("tensor", eng, k)
            dry = Pr.dry

            def mm(ps, lhsT, rhs, start, stop, inckey=None):
                instr = None if dry else eng.matmul(ps, lhsT, rhs, start=start, stop=stop)
                if inckey is not None:
                    Pr.inc(instr, "pe", inckey, 1)
                return instr

            tcur = [0]

            def take_tile(c):
                t = tcur[0]
                tcur[0] += 1
                if t % 16 == 0:
                    wait(("wslot", c, (t % NT) // 16))
                return ring_tile(t)

            wait(("ones",))
            wait(("selinit",))
            for c in range(nch):
                assert tcur[0] == c * NT
                wait(("xin", c))
                if c > 0:
                    wait(("bankfree", c - 1, G_ROT[("w_delta", ECN - 2)]))
                    wait(("bankfree", c - 1, G_ROT[("w_delta", ECN - 1)]))
                for fc in range(FC):
                    mm(psA[:], onescf[:], xv(fc), fc == 0, fc == FC - 1,
                       ("peSx", c) if fc == FC - 1 else None)
                for fc in range(FC):
                    wait(("xsq", c, fc))
                    mm(psB[:], onescf[:], xsqv(fc), fc == 0, fc == FC - 1,
                       ("peSx2f", c, fc))
                Pr.alias(("peSx2", c), ("peSx2f", c, FC - 1))
                wait(("rstd1", c))
                if c > 0:
                    wait(("psyfree", c - 1, TEN_Y0 + ECN - 2))
                mm(psY[0][:], onesrf[:], rowv(1), True, True, ("pebc", c, 0))
                wait(("mr1", c))
                if c > 0:
                    wait(("psyfree", c - 1, TEN_Y0 + ECN - 1))
                mm(psY[1][:], onesrf[:], rowv(0), True, True, ("pebc", c, 1))
                wait(("rstd2", c))
                wait(("psyfree", c, 0))
                mm(psY[0][:], onesrf[:], rowv(1), True, True, ("pebc", c, 2))

                yg_next = [0]

                def emit_y_ng():
                    g = yg_next[0]
                    if g >= 4 * ECN:
                        return False
                    ec, ng = g // 4, g % 4
                    ten = TEN_Y0 + ec
                    bank = ec % 2
                    if ng == 0:
                        wait(("psyfree", c, ten - 2))
                    wait(("q4", c, ec, ng))
                    slot = g % 4
                    for j in range(4):
                        n = ng * 4 + j
                        key = None
                        if n == NST - 1:
                            key = ("pey", c, ec)
                        elif j == 3:
                            key = ("peyg", c, ec, ng)
                        mm(psY[bank][:], ident[:], h4v(slot)[:, j * TC:(j + 1) * TC],
                           n == 0, n == NST - 1, key)
                    if ng == 3:
                        Pr.alias(("peyg", c, ec, 3), ("pey", c, ec))
                    yg_next[0] += 1
                    return True

                for stage, oc in PE_GROUPS:
                    K = GROUP_K[stage]
                    gkey = ("peg", c, stage, oc)
                    ps = None
                    if stage == "w_delta":
                        r = G_ROT[(stage, oc)]
                        if oc >= 2:
                            wait(("bankfree", c, G_ROT[("w_delta", oc - 2)]))
                        elif oc == 0:
                            wait(("gatefree0", c))
                        else:
                            wait(("gatefree1", c))
                        ps = psA if oc % 2 == 0 else psB
                    elif stage in MAIN_STAGES:
                        r = G_ROT[(stage, oc)]
                        r2 = R2[(stage, oc)]
                        if r2 >= 4:
                            wait(("bankfree", c, G_ROT[NONWD[r2 - 4]]))
                        elif c > 0:
                            wait(("bankfree", c - 1, G_ROT[NONWD[NR2 - 4 + r2]]))
                        ps = psM[r2 % 4]
                    if stage == "w_in":
                        wait(("xln", c))
                        rhs = [xlnv(kc) for kc in range(FC)]
                    elif stage == "w_b":
                        for kc in range(ECN):
                            wait(("xc", c, kc))
                        wait(("var1", c))
                        rhs = [xcv(kc) for kc in range(ECN)]
                        ps = psA
                    elif stage == "w_c":
                        wait(("var1", c))
                        rhs = [xcv(kc) for kc in range(ECN)]
                        ps = psB
                    elif stage == "gate":
                        rhs = [xnv(kc) for kc in range(FC)]
                        ps = psA if oc == 0 else psB
                        wait(("bpadev", c) if oc == 0 else ("cpadev", c))
                    elif stage == "w_delta":
                        rhs = [xcv(kc) for kc in range(ECN)]
                    elif stage == "w1e0":
                        wait(("mx0", c))
                        rhs = [xlnv(kc) for kc in range(FC)]
                    elif stage == "w1e1":
                        wait(("mx1", c))
                        rhs = [xlnv(kc) for kc in range(FC)]
                    elif stage == "sw1":
                        rhs = [xnv(kc) for kc in range(FC)]
                    elif stage in ("w2e0h0", "w2e0h1", "w2e1h0", "w2e1h1",
                                   "sw2h0", "sw2h1"):
                        rhs = []
                        for kc in range(16):
                            wait(("h1", c, stage, kc))
                            rhs.append(poolv(kc, 0, TC))
                    elif stage == "w_out":
                        while yg_next[0] < 4 * ECN:
                            emit_y_ng()
                        for ec in range(ECN):
                            wait(("yg", c, ec))
                        rhs = [szv(kc) for kc in range(ECN)]
                    for kc in range(K):
                        lt = take_tile(c)
                        mm(ps[:], lt, rhs[kc], kc == 0, kc == K - 1,
                           gkey if kc == K - 1 else None)
                    if tcur[0] % 16 == 0:
                        Pr.alias(("wfree_g", (tcur[0] - 1) // 16), gkey)

                    if stage == "w_c":
                        wait(("bpadev", c))
                        for n in range(NST):
                            if n >= 2:
                                wait(("psyfree", c, 4 + n - 2))
                            else:
                                wait(("psyfree", c, 2 + n))
                            mm(psY[n % 2][:], selv(n), bpad[:], True, True,
                               ("pebr", c, n))
                        wait(("cpadev", c))
                        for n in range(NST):
                            wait(("psyfree", c, 20 + n - 2))
                            mm(psY[n % 2][:], selv(n), cpad[:], True, True,
                               ("pecr", c, n))
                    if stage == "gate" and oc == 1:
                        wait(("mask0", c))
                        wait(("psyfree", c, 34))
                        mm(psY[0][:], onesrf[:], rowv(2), True, True, ("pebc", c, 4))
                        wait(("mask1", c))
                        wait(("psyfree", c, 35))
                        mm(psY[1][:], onesrf[:], rowv(0), True, True, ("pebc", c, 5))
                    if stage == "w_delta" and oc >= 1:
                        target = 4 * (oc - 1)
                        while yg_next[0] < min(target, 4 * ECN):
                            if not emit_y_ng():
                                break
                while yg_next[0] < 4 * ECN:
                    emit_y_ng()

        def emit_yevac(Pr, eng, op, wait, c, ec):
            wait(("pey", c, ec))
            wait(("bankfree", c, G_ROT[("w_in", ECN + ec)]))
            bank = ec % 2
            op("stt", scrYv(ec % 2), xcv(ec), smc("Dcol", ec), psY[bank][:],
               op0=ALU.mult, op1=ALU.add)
            op("stt", szv(ec), scrYv(ec % 2), 1.0, szv(ec), op0=ALU.mult,
               op1=ALU.mult, inckey=("yg", c, ec))
            Pr.alias(("psyfree", c, TEN_Y0 + ec), ("yg", c, ec))

        def emit_vector(Pr, eng):
            wait = lambda k: Pr.wait("vector", eng, k)
            dry = Pr.dry

            def op(kind, *a, inckey=None, **kw):
                if dry:
                    instr = None
                else:
                    f = {"stt": eng.scalar_tensor_tensor, "ts": eng.tensor_scalar,
                         "cp": eng.tensor_copy, "scan": eng.tensor_tensor_scan,
                         "ms": eng.memset}[kind]
                    instr = f(*a, **kw)
                if inckey is not None:
                    Pr.inc(instr, "ve", inckey, 1)
                return instr

            wait(("iotas",))
            wait(("sm",))
            for n in range(NST):
                op("ts", selv(n), iop[:].broadcast_to([P, P]), float(n), None,
                   op0=ALU.is_equal)
            op("ts", ident[:], scrA[:, 0:P], iop[:, 0:1], None, op0=ALU.is_equal,
               inckey=("selinit",))
            op("ms", onescf[:, 1:P], 0.0)
            op("ms", onescf[:, 0:1], 1.0)
            op("ms", onescb[:, 1:P], 0.0)
            op("ms", onescb[:, 0:1], 1.0)
            op("ms", onesrf[:], 1.0)
            for _i, _v in enumerate(CONSTS):
                op("ms", consts[:, _i:_i + 1], _v,
                   inckey=("cinit",) if _i == len(CONSTS) - 1 else None)
            op("ms", carry[:], 0.0)
            for ec in range(ECN):
                op("ms", poolv(ec, 0, 3), 0.0, inckey=("minit", ec) if ec == ECN - 1 else None)
            Pr.alias(("ones",), ("minit", ECN - 1))
            op("stt", smc("bsum", 0, FC), smc("b2e0", 0, FC), 1.0, smc("b2e1", 0, FC),
               op0=ALU.mult, op1=ALU.add)
            op("stt", smc("bsum", 0, FC), smc("bsum", 0, FC), 1.0, smc("sb2", 0, FC),
               op0=ALU.mult, op1=ALU.add, inckey=("bsum",))
            wait(("acolsexp",))
            op("ts", acols[:], acols[:], -1.0, None, op0=ALU.mult,
               inckey=("acolsneg",))
            for i in range(NPREP):
                wait(("prepin", i))
                if i >= 1:
                    wait(("prepout", i - 1))
                op("cp", xnb[:, 0:SL], xbuf[:, 0:SL], inckey=("prepcast", i))

            for c in range(nch):
                for ec in range(ECN):
                    if c >= 1:
                        wait(("peg", c - 1, "sw2h1", 7))
                        op("cp", poolv(ec, 0, 3), poolv(ec, 512, 515),
                           inckey=("tail", c, ec))
                    else:
                        Pr.alias(("tail", c, ec), ("minit", ECN - 1))
                if c == 0:
                    wait(("prepout", NPREP - 1))
                wait(("peSx", c))
                wait(("peSx2", c))
                if c > 0:
                    wait(("pebc", c - 1, 4))
                    wait(("pebc", c - 1, 5))
                op("ts", rowv(0), psA[0:1, :], INV_D, None, op0=ALU.mult)
                op("ts", rowv(1), psB[0:1, :], INV_D, None, op0=ALU.mult)
                op("stt", rowv(2), rowv(0), 1.0, rowv(0), op0=ALU.mult, op1=ALU.mult)
                op("stt", rowv(1), rowv(1), 1.0, rowv(2), op0=ALU.mult,
                   op1=ALU.subtract, inckey=("var1", c))
                wait(("rstd1", c))
                op("stt", rowv(0), rowv(0), 1.0, rowv(1), op0=ALU.mult, op1=ALU.mult,
                   inckey=("mr1", c))
                wait(("pebc", c, 0))
                op("cp", repv(0), psY[0][:], inckey=("psyfree", c, 0))
                wait(("pebc", c, 1))
                op("cp", repv(1), psY[1][:], inckey=("psyfree", c, 1))
                Pr.alias(("psyfree", c, 3), ("psyfree", c, 1))
                if c >= 1:
                    wait(("peg", c - 1, "sw1", 31))
                for fc in range(FC):
                    op("stt", scrYv(fc % 2), xv(fc), 1.0, repv(0), op0=ALU.mult,
                       op1=ALU.mult)
                    op("stt", xnv(fc), scrYv(fc % 2), 1.0, repv(1), op0=ALU.mult,
                       op1=ALU.subtract, inckey=("xnfc", c, fc))
                Pr.alias(("xn", c), ("xnfc", c, FC - 1))
                wait(("pebc", c, 2))
                op("cp", repv(0), psY[0][:], inckey=("psyfree", c, 2))
                if c >= 1:
                    wait(("peg", c - 1, "w1e1", 31))
                for fc in range(FC):
                    op("stt", scrYv(fc % 2), xnv(fc), 1.0, repv(0), op0=ALU.mult,
                       op1=ALU.mult)
                    op("ts", xlnv(fc), scrYv(fc % 2), smc("wnorm", fc), smc("bnorm", fc),
                       op0=ALU.mult, op1=ALU.add,
                       inckey=("xln", c) if fc == FC - 1 else None)
                if c >= 1:
                    wait(("gsilu", c - 1))
                    wait(("xre", c - 1))
                    for fc in range(FC):
                        op("stt", scrYv(fc % 2), sv(fc), 1.0, gv(fc), op0=ALU.mult,
                           op1=ALU.mult)
                        op("stt", xv(fc), scrYv(fc % 2), 1.0, xv(fc), op0=ALU.mult,
                           op1=ALU.add,
                           inckey=("combine", c - 1) if fc == FC - 1 else None)
                for ec in range(ECN):
                    wait(("xpev", c, ec))
                    wait(("peSx2", c))
                    if ec >= 2:
                        wait(("xc", c, ec - 2))
                    op("ts", scrAv(ec % 2), poolv(ec, 0, TC), smc("w4", ec * 4 + 0),
                       smc("bconv", ec), op0=ALU.mult, op1=ALU.add)
                    for k in range(1, 4):
                        op("stt", scrAv(ec % 2), poolv(ec, k, TC + k),
                           smc("w4", ec * 4 + k), scrAv(ec % 2),
                           op0=ALU.mult, op1=ALU.add,
                           inckey=("conv", c, ec) if k == 3 else None)
                wait(("peg", c, "gate", 0))
                wait(("peg", c, "gate", 1))
                op("cp", rowv(0), psA[0:1, :], inckey=("gatefree0", c))
                op("cp", rowv(1), psB[0:1, :], inckey=("gatefree1", c))
                op("stt", rowv(2), rowv(0), 1.0, rowv(1), op0=ALU.mult,
                   op1=ALU.subtract)
                op("ts", rowv(2), rowv(2), 0.0, None, op0=ALU.is_ge,
                   inckey=("mask0", c))
                op("ts", rowv(0), rowv(2), -1.0, 1.0, op0=ALU.mult, op1=ALU.add,
                   inckey=("mask1", c))
                wait(("pebc", c, 4))
                op("cp", maskv(0), psY[0][:], inckey=("psyfree", c, 36))
                wait(("pebc", c, 5))
                op("cp", maskv(1), psY[1][:], inckey=("psyfree", c, 37))
                wait(("peg", c, "w_in", 31))
                for fc in range(FC):
                    op("stt", xlnv(fc), xnv(fc), 1.0, maskv(0), op0=ALU.mult,
                       op1=ALU.mult, inckey=("mx0", c) if fc == FC - 1 else None)

                tasks = []
                for stage, oc in PE_GROUPS:
                    if stage in ("w2e0h1", "w2e1h0", "w2e1h1", "sw2h0", "sw2h1"):
                        tasks.append((GPOS[(stage, oc)], stage, oc))
                ti = [0]

                def drain_tasks(upto_pos):
                    while ti[0] < len(tasks) and tasks[ti[0]][0] <= upto_pos:
                        _, stage, oc = tasks[ti[0]]
                        r = G_ROT[(stage, oc)]
                        wait(("peg", c, stage, oc))
                        wait(("bankfree", c, G_ROT[("w2e0h0", oc)]))
                        op("stt", gv(oc), gv(oc), 1.0, psM[R2[(stage, oc)] % 4][:],
                           op0=ALU.mult, op1=ALU.add, inckey=("bankfree", c, r))
                        ti[0] += 1

                def emit_carry(sec, ng):
                    op("cp", carry[:, sec * NST + ng * 4: sec * NST + ng * 4 + 4],
                       h4v(ng).rearrange("p (n t) -> p n t", n=4)[:, :, TC - 1],
                       inckey=("h4", c, sec, ng))

                def b4_emit(bec, ng):
                    u_b = uv(bec % 2).unsqueeze(1).broadcast_to([P, 4, TC])
                    for n in range(ng * 4, ng * 4 + 4):
                        wait(("brevac", c, n))
                    op("stt", b4v(ng % 2).rearrange("p (g t) -> p g t", g=4),
                       u_b, 1.0,
                       brep[:, ng * 4 * TC:(ng + 1) * 4 * TC].rearrange(
                           "p (g t) -> p g t", g=4),
                       op0=ALU.mult, op1=ALU.mult, inckey=("b4", c, bec, ng))

                def q4_emit(sec, ng):
                    for n in range(ng * 4, ng * 4 + 4):
                        wait(("crevac", c, n))
                    op("stt", h4v(ng).rearrange("p (g t) -> p g t", g=4),
                       crep[:, ng * 4 * TC:(ng + 1) * 4 * TC].rearrange(
                           "p (g t) -> p g t", g=4), 1.0,
                       h4v(ng).rearrange("p (g t) -> p g t", g=4),
                       op0=ALU.mult, op1=ALU.mult, inckey=("q4", c, sec, ng))

                def emit_scans(sec):
                    have_next = hasattr(emit_scans, "post_g1")
                    for ng in range(4):
                        g = sec * 4 + ng
                        gg = c * 4 * ECN + g
                        if gg >= 4:
                            wait(peygkey(gg - 4))
                        for j in range(4):
                            n = ng * 4 + j
                            gn = sec * NST + n
                            wait(("a", c, sec, n))
                            op("scan", h4v(ng)[:, j * TC:(j + 1) * TC],
                               arv(gn % 8), b4v(ng % 2)[:, j * TC:(j + 1) * TC],
                               carry[:, sec * NST + n: sec * NST + n + 1],
                               op0=ALU.mult, op1=ALU.add)
                        pg = sec * 4 + ng - 1
                        if pg >= 0:
                            q4_emit(pg // 4, pg % 4)
                        if ng == 0:
                            b4_emit(sec, 2)
                        elif ng == 1:
                            if have_next:
                                emit_scans.post_g1()
                            b4_emit(sec, 3)
                        elif have_next:
                            b4_emit(sec + 1, ng - 2)
                        emit_carry(sec, ng)

                for ec in range(ECN):
                    drain_tasks(WD_POS[ec])
                    r = G_ROT[("w_delta", ec)]
                    wait(("peg", c, "w_delta", ec))
                    gec = c * ECN + ec
                    if gec >= 1:
                        c2, e2 = divmod(gec - 1, ECN)
                        wait(("spln", c2, e2))
                    if gec >= 3:
                        c2, e2 = divmod(gec - 3, ECN)
                        wait(("a", c2, e2, NST - 1))
                    op("ts", spv(0), (psA if ec % 2 == 0 else psB)[:],
                       smc("bdelta", ec), None,
                       op0=ALU.add, inckey=("bankfree", c, r))
                    op("ts", spv(1), spv(0), 0.0, None, op0=ALU.max)
                    op("stt", spv(2), spv(0), -1.0, spv(0), op0=ALU.mult,
                       op1=ALU.min, inckey=("spna", c, ec))
                    mx1_ec = next(e for e in range(ECN)
                                  if WD_POS[e] > GPOS[("w1e0", 31)])
                    assert WD_POS[mx1_ec] < GPOS[("w1e1", 0)]
                    if ec == mx1_ec:
                        wait(("peg", c, "w1e0", 31))
                        if DBG_TAPS and c == TAPC:
                            wait(("tapmx0",))
                        for fc in range(FC):
                            op("stt", xlnv(fc), xnv(fc), 1.0, maskv(1), op0=ALU.mult,
                               op1=ALU.mult,
                               inckey=("mx1", c) if fc == FC - 1 else None)
                    def sp2u(ec=ec, gec=gec):
                        wait(("spln", c, ec))
                        op("stt", dlv(ec), spv(1), 1.0, spv(2), op0=ALU.mult,
                           op1=ALU.add, inckey=("delta", c, ec))
                        wait(("xc", c, ec))
                        op("stt", uv(ec % 2), dlv(ec), 1.0, xcv(ec), op0=ALU.mult,
                           op1=ALU.mult, inckey=("u", c, ec))
                    if ec >= 1:
                        emit_scans.post_g1 = sp2u
                        emit_scans(ec - 1)
                        del emit_scans.post_g1
                    else:
                        sp2u()
                        b4_emit(0, 0)
                        b4_emit(0, 1)
                    if ec >= 2:
                        emit_yevac(Pr, eng, op, wait, c, ec - 2)
                emit_scans(ECN - 1)
                q4_emit(ECN - 1, 3)
                for ec in (ECN - 2, ECN - 1):
                    emit_yevac(Pr, eng, op, wait, c, ec)
                drain_tasks(10 ** 9)
                for oc in range(8):
                    r = G_ROT[("w_out", oc)]
                    wait(("peg", c, "w_out", oc))
                    op("stt", sv(oc), xnv(oc), 1.0,
                       psM[R2[("w_out", oc)] % 4][:], op0=ALU.mult,
                       op1=ALU.add, inckey=("bankfree", c, r))
                    if oc == 7:
                        Pr.alias(("schunk", c), ("bankfree", c, r))
            c = nch
            wait(("gsilu", c - 1))
            wait(("xre", c - 1))
            for fc in range(FC):
                op("stt", scrYv(fc % 2), sv(fc), 1.0, gv(fc), op0=ALU.mult,
                   op1=ALU.mult)
                op("stt", xv(fc), scrYv(fc % 2), 1.0, xv(fc), op0=ALU.mult,
                   op1=ALU.add,
                   inckey=("combine", c - 1) if fc == FC - 1 else None)

        def emit_scalar(Pr, eng):
            wait = lambda k: Pr.wait("scalar", eng, k)
            dry = Pr.dry

            def op(kind, *a, inckey=None, **kw):
                if dry:
                    instr = None
                else:
                    f = {"act": eng.activation, "cp": eng.copy}[kind]
                    instr = f(*a, **kw)
                if inckey is not None:
                    Pr.inc(instr, "se", inckey, 1)
                return instr

            wait(("sm",))
            wait(("cinit",))
            op("act", acols[:], smc("Alog", 0, ECN * NST), AF.Exp,
               inckey=("acolsexp",))

            for c in range(nch):
                wait(("xin", c))
                if c == 0:
                    wait(("selinit",))
                if c > 0:
                    wait(("pebc", c - 1, 4))
                    wait(("pebc", c - 1, 5))
                if c > 0:
                    wait(("peg", c - 1, "w_out", 7))
                    wait(("schunk", c - 1))
                for fc in range(FC):
                    if fc >= 4:
                        wait(("peSx2f", c, fc - 4))
                    op("act", xsqv(fc), xv(fc), AF.Square,
                       inckey=("xsq", c, fc))
                wait(("var1", c))
                op("act", rowv(2), rowv(1), AF.Ln, bias=EPS)
                op("act", rowv(1), rowv(2), AF.Exp, scale=-0.5,
                   inckey=("rstd1", c))
                wait(("pebc", c, 0))
                wait(("mr1", c))
                op("act", rowv(2), rowv(2), AF.Exp, scale=-1.0, bias=LOG_EPS)
                op("act", rowv(2), rowv(2), AF.Ln, scale=-1.0, bias=1.0 + EPS)
                op("act", rowv(1), rowv(2), AF.Exp, scale=-0.5,
                   inckey=("rstd2", c))
                if c >= 1:
                    wait(("w2done", c - 1))
                    wait(("combine2", c - 1)) if False else None
                    for fc in range(FC):
                        op("act", gv(fc), gv(fc), AF.Silu, bias=smc("bsum", fc),
                           inckey=("gsilu", c - 1) if fc == FC - 1 else None)
                for oc in range(32):
                    r = G_ROT[("w_in", oc)]
                    wait(("peg", c, "w_in", oc))
                    if oc < ECN:
                        wait(("tail", c, oc))
                        op("cp", poolv(oc, 3, 515), psM[R2[("w_in", oc)] % 4][:],
                           inckey=("bankfree", c, r))
                        Pr.alias(("xpev", c, oc), ("bankfree", c, r))
                    else:
                        if c >= 1:
                            wait(("peg", c - 1, "w_out", 7))
                        op("act", szv(oc - ECN), psM[R2[("w_in", oc)] % 4][:],
                           AF.Silu, inckey=("bankfree", c, r))
                for ec in range(ECN):
                    wait(("conv", c, ec))
                    if c >= 1:
                        wait(("peg", c - 1, "w_delta", 15))
                        wait(("yg", c - 1, ec))
                    op("act", xcv(ec), scrAv(ec % 2), AF.Silu,
                       inckey=("xc", c, ec))
                wait(("peg", c, "w_b", 0))
                if c >= 1:
                    wait(("pebr", c - 1, NST - 1))
                op("cp", bpad[:], psA[:], inckey=("bpadev", c))
                wait(("peg", c, "w_c", 0))
                if c >= 1:
                    wait(("pecr", c - 1, NST - 1))
                op("cp", cpad[:], psB[:], inckey=("cpadev", c))
                for n in range(NST):
                    wait(("pebr", c, n))
                    if c >= 1:
                        wait(("b4", c - 1, ECN - 1, n // 4))
                    op("cp", brv(n), psY[n % 2][:], inckey=("brevac", c, n))
                    Pr.alias(("psyfree", c, 4 + n), ("brevac", c, n))
                for n in range(NST):
                    wait(("pecr", c, n))
                    if c >= 1:
                        wait(("q4", c - 1, ECN - 1, n // 4))
                    op("cp", crv(n), psY[n % 2][:], inckey=("crevac", c, n))
                    Pr.alias(("psyfree", c, 20 + n), ("crevac", c, n))
                pend_a = []

                def emit_a_group(ec, ng):
                    wait(("delta", c, ec))
                    wait(("acolsneg",))
                    for n in range(ng * 4, ng * 4 + 4):
                        gan = (c * ECN + ec) * NST + n
                        if gan >= 8:
                            prevg = (gan - 8) // 4
                            wait(sgkey(prevg))
                        op("act", arv((ec * NST + n) % 8), dlv(ec), AF.Exp,
                           scale=acols[:, ec * NST + n: ec * NST + n + 1],
                           inckey=("a", c, ec, n))

                def drain_a(k=1):
                    while k > 0 and pend_a:
                        emit_a_group(*pend_a.pop(0))
                        k -= 1

                def emit_a(ec):
                    for ng in range(4):
                        emit_a_group(ec, ng)

                for stage, oc in PE_GROUPS:
                    if stage == "w_delta":
                        ec = oc
                        wait(("spna", c, ec))
                        op("act", spv(0), spv(2), AF.Exp)
                        op("act", spv(2), spv(0), AF.Ln, bias=1.0,
                           inckey=("spln", c, ec))
                        if ec >= 1:
                            emit_a(ec - 1)
                    elif stage in ("w1e0", "w1e1", "sw1"):
                        drain_a(1)
                        kc = oc % 16
                        half = oc // 16
                        r = G_ROT[(stage, oc)]
                        wait(("peg", c, stage, oc))
                        if stage == "w1e0" and half == 0:
                            wait(("conv", c, kc))
                        else:
                            wait(("peg", c, H1_PREV[(stage, half)], 7))
                        bname = {"w1e0": "b1e0", "w1e1": "b1e1", "sw1": "sb1"}[stage]
                        path = {"w1e0": "e0", "w1e1": "e1", "sw1": "sh"}[stage]
                        if DBG_TAPS and c == TAPC and (stage, half) in H1_PREV:
                            wait(("tapdma", (stage, half), kc))
                        op("act", poolv(kc, 0, TC), psM[R2[(stage, oc)] % 4][:],
                           AF.Relu, bias=smc(bname, half * 16 + kc),
                           inckey=("bankfree", c, r))
                        Pr.alias(("h1", c, W2_OF[path] + f"h{half}", kc),
                                 ("bankfree", c, r))
                    elif stage == "w2e0h0":
                        drain_a(1)
                        r = G_ROT[(stage, oc)]
                        wait(("peg", c, stage, oc))
                        if c >= 1 and oc == 0:
                            wait(("combine", c - 1))
                        op("cp", gv(oc), psM[R2[(stage, oc)] % 4][:],
                           inckey=("bankfree", c, r))
                drain_a(99)
                emit_a(ECN - 1)
                Pr.alias(("w2done", c), ("bankfree", c, G_ROT[("sw2h1", 7)]))
            c = nch
            wait(("w2done", c - 1))
            for fc in range(FC):
                op("act", gv(fc), gv(fc), AF.Silu, bias=smc("bsum", fc),
                   inckey=("gsilu", c - 1) if fc == FC - 1 else None)

        def emit_all(Pr, engs):
            emit_sync(Pr, engs.get("sync"))
            emit_gpsimd(Pr, engs.get("gpsimd"))
            emit_tensor(Pr, engs.get("tensor"))
            emit_vector(Pr, engs.get("vector"))
            emit_scalar(Pr, engs.get("scalar"))

        dryP = Prog(dry=True)
        emit_all(dryP, {})
        realP = Prog(dry=False, events=dryP.events, sems=sems)

        @block.sync
        def _(sync):
            emit_sync(realP, sync)

        @block.gpsimd
        def _(g):
            emit_gpsimd(realP, g)

        @block.tensor
        def _(t):
            emit_tensor(realP, t)

        @block.vector
        def _(v):
            emit_vector(realP, v)

        @block.scalar
        def _(s):
            emit_scalar(realP, s)

        for k in ("sy", "pe", "ve", "se", "po", "pd"):
            assert dryP.cnt[k] == realP.cnt[k], (k, dryP.cnt[k], realP.cnt[k])
    return nc


def kernel(**inputs):
    x = np.asarray(inputs["L0_attn_x"])
    wt = host_pack_wtiles(inputs)
    sm = host_pack_smalls(inputs)
    nch = x.shape[1] // TC
    nc = build_nc(nch)
    in_maps = []
    for b in range(x.shape[0]):
        in_maps.append({
            "xT": np.ascontiguousarray(x[b].T),
            "wt": wt,
            "smI": sm,
        })
    res = run_bass_kernel_spmd(nc, in_maps, core_ids=list(range(x.shape[0])))
    out = np.stack([np.asarray(res.results[b]["outT"]).T for b in range(x.shape[0])], axis=0)
    return out
